# version 27
# baseline (speedup 1.0000x reference)
import sys
sys.path.insert(0, "/opt/trn_rl_repo")
import contextlib
import numpy as np
import concourse.bass as bass
import concourse.mybir as mybir
from concourse.bass_utils import run_bass_kernel_spmd

F32 = mybir.dt.float32
BF16 = mybir.dt.bfloat16
AF = mybir.ActivationFunctionType
OP = mybir.AluOpType
AX = mybir.AxisListType

P = 128
D = 1024
E = 2048
S = 2048
BC = 2
NM = 8
STK = 512
NST = 4
DC = D // P
ECN = E // P
EPS = 1e-5
DBG = False
DBG_F = -1
SERIAL = False

ENGINES = ("sync", "tensor", "vector", "scalar", "gpsimd")
OWN_SEM = {"sync": "in_s", "tensor": "t_s", "vector": "v_s",
           "scalar": "s_s", "gpsimd": "g_s"}
ALL_SEMS = ("in_s", "out_s", "w_s", "g_s", "t_s", "v_s", "s_s")


class Sched:

    def __init__(self):
        self.ops = {e: [] for e in ENGINES}
        self.count = {s: 0 for s in ALL_SEMS}
        self.last_w = {}
        self.readers = {}
        self.warm_fn = None
        self._vn = 0
        self._in_warm = False

    @staticmethod
    def _key(buf):
        if isinstance(buf, tuple):
            return (id(buf[0]), buf[1])
        return id(buf)

    def add(self, engine, fn, reads=(), writes=(), deps=(), sem=None, inc=None,
            hard=(), war_reads=()):
        if sem is None:
            sem = OWN_SEM[engine]
        if inc is None:
            inc = 16 if engine == "sync" else 1
        deps = [d for d in deps if d is not None]
        for b in reads:
            t = self.last_w.get(self._key(b))
            if t is not None:
                deps.append(t)
        for b in writes:
            k = self._key(b)
            t = self.last_w.get(k)
            if t is not None:
                deps.append(t)
            for s, v in self.readers.get(k, {}).items():
                deps.append((s, v))
        hard = [d for d in hard if d is not None]
        self.count[sem] += inc
        tok = (sem, self.count[sem])
        self.ops[engine].append((tuple(deps), tuple(hard), fn, sem, inc))
        for b in list(reads) + list(war_reads):
            k = self._key(b)
            rd = self.readers.setdefault(k, {})
            rd[sem] = self.count[sem]
        for b in writes:
            k = self._key(b)
            if inc:
                self.last_w[k] = tok
            self.readers[k] = {}
        if (engine == "vector" and self.warm_fn is not None
                and not self._in_warm):
            self._vn += 1
            if self._vn % 2 == 0:
                self._in_warm = True
                self.warm_fn(tok)
                self._in_warm = False
        return tok

    def emit(self, engine, eng, sems):
        waited = {}
        for deps, hard, fn, sem, inc in self.ops[engine]:
            todo = {}
            for (s, v) in list(deps) + list(hard):
                if v > todo.get(s, 0):
                    todo[s] = v
            for s, v in sorted(todo.items()):
                if v <= waited.get(s, 0):
                    continue
                eng.wait_ge(sems[s], v)
                waited[s] = v
            inst = fn(eng)
            if inc and inst is not None:
                inst.then_inc(sems[sem], inc)


def r3(ap, pattern, **kw):
    return ap.rearrange(pattern, **kw)


def build_nc():
    nc = bass.Bass("TRN2", target_bir_lowering=False,
                   dynamic_dma_scratch_size=6144)
    x_h = nc.dram_tensor("x", [BC, S, D], F32, kind="ExternalInput").ap()
    whg_h = nc.dram_tensor("whgT", [D, E], F32, kind="ExternalInput").ap()
    wr_h = nc.dram_tensor("wrT", [D, D], F32, kind="ExternalInput").ap()
    wk_h = nc.dram_tensor("wkT", [D, D], F32, kind="ExternalInput").ap()
    wv_h = nc.dram_tensor("wvT", [D, D], F32, kind="ExternalInput").ap()
    mur_h = nc.dram_tensor("mur", [D], F32, kind="ExternalInput").ap()
    muk_h = nc.dram_tensor("muk", [D], F32, kind="ExternalInput").ap()
    y_h = nc.dram_tensor("y", [BC, S, D], F32, kind="ExternalOutput").ap()

    with contextlib.ExitStack() as es:
        sb = lambda name, shape, dt: es.enter_context(nc.sbuf_tensor(name, shape, dt))

        w_hg = sb("w_hg", [P, DC * E], BF16)
        w_r = sb("w_r", [P, DC * D], BF16)
        w_k = sb("w_k", [P, DC * D], BF16)
        w_v = sb("w_v", [P, DC * D], BF16)
        mur_sb = sb("mur_sb", [P, DC], F32)
        muk_sb = sb("muk_sb", [P, DC], F32)
        id32 = sb("id32", [P, P], F32)
        id16 = sb("id16", [P, P], BF16)
        x_tms = [sb("x_tm0", [P, NST * D], F32), sb("x_tm1", [P, NST * D], F32)]
        bfA = sb("bfA", [P, 4096], BF16)
        xsf = sb("xsf", [P, 4096], BF16)
        zf = sb("zf", [P, 4096], F32)
        rf = sb("rf", [P, 4096], F32)
        mf = sb("mf", [P, 4096], F32)
        B1 = sb("B1", [P, 4096], F32)
        B2 = sb("B2", [P, 4096], F32)
        xkb = sb("xkb", [P, 4096], BF16)
        stats = sb("stats", [P, 48], F32)
        red1 = sb("red1", [P, 12], F32)
        sqscr = sb("sqscr", [P, 16], F32)
        negm = sb("negm", [P, 4], F32)
        rstd = sb("rstd", [P, 4], F32)
        msq = sb("msq", [P, 4], F32)
        t1v = sb("t1v", [P, 4], F32)
        varv = sb("varv", [P, 4], F32)
        nmr = sb("nmr", [P, 4], F32)
        carry = sb("carry", [P, 2 * DC], F32)
        ylast = sb("ylast", [P, 2 * DC], BF16)

        ps = [es.enter_context(nc.psum_tensor(f"ps{i}", [P, 512], F32))
              for i in range(4)]
        pt = [es.enter_context(nc.psum_tensor(f"pt{i}", [P, 512], F32))
              for i in range(2)]
        pb = [es.enter_context(nc.psum_tensor(f"pb{i}", [P, 512], BF16))
              for i in range(2)]

        sems = {n: es.enter_context(nc.semaphore(n)) for n in ALL_SEMS}

        sch = Sched()
        b2f = [(B2, c) for c in range(DC)]

        def pe_warm(tok):
            sch.add("tensor", lambda eng: eng.ldweights(id16[:]),
                    reads=[id16], deps=[tok])
        sch.warm_fn = pe_warm

        sch.add("gpsimd", lambda eng: eng.memset(id32[:], 0.0), writes=[id32])
        sch.add("gpsimd", lambda eng: eng.affine_select(
            out=id32[:], in_=id32[:], compare_op=OP.not_equal,
            fill=1.0, base=0, pattern=[[-1, P]], channel_multiplier=1),
            reads=[id32], writes=[id32])
        sch.add("gpsimd", lambda eng: eng.memset(id16[:], 0.0), writes=[id16])
        sch.add("gpsimd", lambda eng: eng.affine_select(
            out=id16[:], in_=id16[:], compare_op=OP.not_equal,
            fill=1.0, base=0, pattern=[[-1, P]], channel_multiplier=1),
            reads=[id16], writes=[id16])

        def dma_w(dst, src_ap):
            return lambda eng: eng.dma_start(out=dst, in_=src_ap)
        tok_whg = sch.add("gpsimd", dma_w(r3(w_hg[:], "p (dc e) -> p dc e", dc=DC),
                                          r3(whg_h, "(dc p) e -> p dc e", p=P)),
                          writes=[w_hg], sem="w_s", inc=16)
        tok_wr = sch.add("gpsimd", dma_w(r3(w_r[:], "p (dc o) -> p dc o", dc=DC),
                                         r3(wr_h, "(dc p) o -> p dc o", p=P)),
                         writes=[w_r], sem="w_s", inc=16, hard=[tok_whg])
        tok_wk = sch.add("gpsimd", dma_w(r3(w_k[:], "p (dc o) -> p dc o", dc=DC),
                                         r3(wk_h, "(dc p) o -> p dc o", p=P)),
                         writes=[w_k], sem="w_s", inc=16, hard=[tok_wr])
        tok_wv = sch.add("gpsimd", dma_w(r3(w_v[:], "p (dc o) -> p dc o", dc=DC),
                                         r3(wv_h, "(dc p) o -> p dc o", p=P)),
                         writes=[w_v], sem="w_s", inc=16, hard=[tok_wk])

        def dma_mu(dst, src_ap):
            return lambda eng: eng.dma_start(out=dst, in_=src_ap,
                                             allow_slow_non_contiguous=True)
        tok_mur = sch.add("gpsimd", dma_mu(mur_sb[:], r3(mur_h, "(c p) -> p c", p=P)),
                          writes=[mur_sb], sem="w_s", inc=16, hard=[tok_wv])
        sch.add("gpsimd", dma_mu(muk_sb[:], r3(muk_h, "(c p) -> p c", p=P)),
                writes=[muk_sb], sem="w_s", inc=16, hard=[tok_mur])

        ps_free = [None, None, None, None]
        pt_free = [None, None]
        pb_free = [None, None]
        t_kind = [None, None]

        def t_switch(kind):
            prev_kind, prev_tok = t_kind
            t_kind[0] = kind
            if prev_kind is not None and prev_kind != kind:
                return [prev_tok]
            return []

        def t_done(tok):
            t_kind[1] = tok

        def x_load(m, stg):
            b, q = m // 4, m % 4
            src = r3(x_h[b, q * STK:(q + 1) * STK, :], "(st p) d -> p st d", p=P)
            dst = r3(x_tms[stg % 2][:], "p (st d) -> p st d", d=D)
            sch.add("sync", lambda eng, s=src, d=dst: eng.dma_start(out=d, in_=s),
                    writes=[x_tms[stg % 2]], sem="in_s", inc=16,
                    hard=[("in_s", 16 * stg)] if stg >= 1 else [])

        def ln_stats(src_t):
            src = src_t[:]
            def bn(eng):
                last = None
                for st in range(NST):
                    for hlf in range(2):
                        last = eng.bn_stats(
                            stats[:, st * 12 + hlf * 6: st * 12 + hlf * 6 + 6],
                            src[:, st * D + hlf * 512: st * D + (hlf + 1) * 512])
                return last
            sch.add("vector", bn, reads=[src_t], writes=[stats])

            sv = r3(stats[:], "p (st g t) -> p st t g", st=4, g=4)
            means16 = r3(stats[:], "p (sg t) -> p sg t", t=3)[:, :, 1:2]
            rmean = r3(red1[:], "p (st t) -> p t st", t=3)
            sch.add("vector", lambda eng: eng.tensor_reduce(
                r3(red1[:], "p (st t o) -> p st t o", st=4, o=1), sv,
                axis=AX.X, op=OP.add), reads=[stats], writes=[red1])
            sch.add("vector", lambda eng: eng.tensor_tensor(
                out=r3(sqscr[:], "p (a o) -> p a o", o=1),
                in0=means16, in1=means16, op=OP.mult),
                reads=[stats], writes=[sqscr])
            sch.add("vector", lambda eng: eng.tensor_reduce(
                r3(t1v[:], "p (a o) -> p a o", o=1),
                r3(sqscr[:], "p (st g) -> p st g", g=4),
                axis=AX.X, op=OP.add), reads=[sqscr], writes=[t1v])
            sch.add("vector", lambda eng: eng.tensor_scalar(
                out=r3(negm[:], "p (o st) -> p o st", o=1),
                in0=rmean[:, 1:2, :], scalar1=-0.25, scalar2=None,
                op0=OP.mult), reads=[red1], writes=[negm])
            sch.add("vector", lambda eng: eng.tensor_tensor(
                out=msq[:], in0=negm[:], in1=negm[:], op=OP.mult),
                reads=[negm], writes=[msq])
            sch.add("vector", lambda eng: eng.tensor_scalar(
                out=sqscr[:, 0:4], in0=msq[:], scalar1=-EPS,
                scalar2=None, op0=OP.add), reads=[msq], writes=[sqscr])
            sch.add("vector", lambda eng: eng.scalar_tensor_tensor(
                out=t1v[:], in0=t1v[:], scalar=0.25, in1=sqscr[:, 0:4],
                op0=OP.mult, op1=OP.subtract),
                reads=[t1v, sqscr], writes=[t1v])
            sch.add("vector", lambda eng: eng.scalar_tensor_tensor(
                out=r3(varv[:], "p (o st) -> p o st", o=1),
                in0=rmean[:, 2:3, :], scalar=1.0 / 1024.0,
                in1=r3(t1v[:], "p (o st) -> p o st", o=1),
                op0=OP.mult, op1=OP.add),
                reads=[red1, t1v], writes=[varv])
            sch.add("scalar", lambda eng: eng.activation(
                out=t1v[:], in_=varv[:], func=AF.Ln),
                reads=[varv], writes=[t1v])
            sch.add("scalar", lambda eng: eng.activation(
                out=rstd[:], in_=t1v[:], func=AF.Exp, scale=-0.5),
                reads=[t1v], writes=[rstd])


        def ln_apply(dst_t, src_t, engine="vector"):
            dst, src = dst_t[:], src_t[:]
            dkeys = b2f if dst_t is B2 else [dst_t]
            skeys = b2f if src_t is B2 else [src_t]
            toks = []
            for st in range(NST):
                if engine == "scalar":
                    def apl(eng, st=st):
                        return eng.activation(
                            out=dst[:, st * D:(st + 1) * D],
                            in_=src[:, st * D:(st + 1) * D],
                            func=AF.Identity,
                            scale=rstd[:, st:st + 1], bias=nmr[:, st:st + 1])
                    toks.append(sch.add("scalar", apl,
                                        reads=skeys + [rstd, nmr], writes=dkeys))
                else:
                    def apl(eng, st=st):
                        return eng.tensor_scalar(
                            out=dst[:, st * D:(st + 1) * D],
                            in0=src[:, st * D:(st + 1) * D],
                            scalar1=negm[:, st:st + 1], scalar2=rstd[:, st:st + 1],
                            op0=OP.add, op1=OP.mult)
                    toks.append(sch.add("vector", apl,
                                        reads=skeys + [negm, rstd], writes=dkeys))
            return toks

        def transpose_tm_to_fm(src_t, dst_t, st_deps=None):
            nonlocal pb_free
            src, dst = src_t[:], dst_t[:]
            for st in range(NST):
                for half in range(2):
                    bank = half
                    def tr(eng, st=st, half=half, bank=bank, src=src):
                        last = None
                        for ci in range(4):
                            c = half * 4 + ci
                            last = eng.transpose(
                                pb[bank][:, ci * P:(ci + 1) * P],
                                src[:, st * D + c * P: st * D + (c + 1) * P],
                                id16[:])
                        return last
                    dep = [st_deps[st]] if st_deps else []
                    rd = [id16] if st_deps else [src_t, id16]
                    t_tr = sch.add("tensor", tr, reads=rd,
                                   war_reads=[src_t] if st_deps else [],
                                   deps=dep + [pb_free[bank]],
                                   hard=t_switch("tr"))
                    t_done(t_tr)
                    def ev(eng, st=st, half=half, bank=bank, dst=dst):
                        dv = r3(dst, "p (c s) -> p c s", c=DC)
                        return eng.copy(
                            out=dv[:, half * 4:(half + 1) * 4,
                                   st * P:(st + 1) * P],
                            in_=r3(pb[bank][:], "p (ci s) -> p ci s", ci=4))
                    t_ev = sch.add("scalar", ev,
                                   writes=[(dst_t, c)
                                           for c in range(half * 4, half * 4 + 4)],
                                   deps=[t_tr])
                    pb_free[bank] = t_ev

        def transpose_fm_to_tm(src_t, dst_t, fine=False):
            nonlocal pt_free
            src, dst = src_t[:], dst_t[:]
            for st in range(NST):
                for half in range(2):
                    bank = half
                    def tr(eng, st=st, half=half, bank=bank, src=src):
                        last = None
                        for ci in range(4):
                            c = half * 4 + ci
                            last = eng.transpose(
                                pt[bank][:, ci * P:(ci + 1) * P],
                                src[:, c * STK + st * P: c * STK + (st + 1) * P],
                                id32[:])
                        return last
                    rd = ([(src_t, c) for c in range(half * 4, half * 4 + 4)]
                          if fine else [src_t])
                    t_tr = sch.add("tensor", tr, reads=rd + [id32],
                                   deps=[pt_free[bank]], hard=t_switch("tr"))
                    t_done(t_tr)
                    def ev(eng, st=st, half=half, bank=bank, dst=dst):
                        return eng.copy(
                            out=dst[:, st * D + half * 512: st * D + (half + 1) * 512],
                            in_=pt[bank][:])
                    t_ev = sch.add("scalar", ev, writes=[dst_t], deps=[t_tr])
                    pt_free[bank] = t_ev

        def mm_block(w_t, w_stride, n_out, rhs_t, w_tok, evac_fns):
            nonlocal ps_free
            w_sb, rhs = w_t[:], rhs_t[:]
            for oc in range(n_out):
                bank = oc % 4
                def mm(eng, oc=oc, w_sb=w_sb, w_stride=w_stride, rhs=rhs, bank=bank):
                    last = None
                    for dc in range(DC):
                        last = eng.matmul(
                            ps[bank][:],
                            w_sb[:, dc * w_stride + oc * P: dc * w_stride + (oc + 1) * P],
                            rhs[:, dc * STK:(dc + 1) * STK],
                            start=(dc == 0), stop=(dc == DC - 1))
                    return last
                bank_deps = ps_free[bank] or []
                t_mm = sch.add("tensor", mm, reads=[w_t, rhs_t],
                               deps=[w_tok] + list(bank_deps),
                               hard=t_switch("mm"))
                t_done(t_mm)
                prev_ev = None
                for (engine, fn, rd, wr) in evac_fns(oc):
                    prev_ev = sch.add(
                        engine, (lambda eng, fn=fn, bank=bank: fn(eng, ps[bank][:])),
                        reads=list(rd), writes=list(wr),
                        deps=[t_mm] + ([prev_ev] if prev_ev else []))
                ps_free[bank] = [prev_ev]

        dbg_toks = {}
        def dbg_dump(name, buf, dt=F32):
            if not DBG:
                return
            t = nc.dram_tensor(name, [P, 4096], dt, kind="ExternalOutput").ap()
            sch.add("sync", lambda eng, t=t, buf=buf: eng.dma_start(out=t, in_=buf[:]),
                    reads=[buf], sem="out_s", inc=16)
            dbg_toks[name] = True


        ln1_toks = [None]
        ln3_toks = [None]

        def front_LN1(f, stg):
            ln_stats(x_tms[stg % 2])
            ln1_toks[0] = ln_apply(bfA, x_tms[stg % 2])

        def back_hT(m):
            transpose_fm_to_tm(zf, B1, fine=True)

        def front_xsT(f):
            transpose_tm_to_fm(bfA, xsf, st_deps=ln1_toks[0])

        def back_LN2(m, stg):
            x_tm = x_tms[stg % 2]
            ln_stats(B1)
            ln_apply(B2, B1, engine="vector")
            sch.add("vector", lambda eng, x_tm=x_tm: eng.tensor_tensor(
                out=x_tm[:], in0=x_tm[:], in1=B2[:], op=OP.add),
                reads=[x_tm] + b2f, writes=[x_tm])

        def front_attn_scan(f):
            nonlocal ps_free
            first = (f % 4 == 0)
            cb = (f // 4) * DC
            w_sb, rhs = w_hg[:], xsf[:]

            def mm_group(ec, bank):
                def mm(eng, ec=ec, bank=bank):
                    last = None
                    for dc in range(DC):
                        last = eng.matmul(
                            ps[bank][:],
                            w_sb[:, dc * E + ec * P: dc * E + (ec + 1) * P],
                            rhs[:, dc * STK:(dc + 1) * STK],
                            start=(dc == 0), stop=(dc == DC - 1))
                    return last
                bank_deps = ps_free[bank] or []
                return sch.add("tensor", mm,
                               reads=[w_hg] + [(xsf, cc) for cc in range(DC)],
                               deps=[tok_whg] + list(bank_deps),
                               hard=t_switch("mm"))

            for c in range(DC):
                bank_h = (2 * c) % 4
                bank_g = (2 * c + 1) % 4
                t_mm_h = mm_group(c, bank_h)
                t_done(t_mm_h)
                t_mm_g = mm_group(DC + c, bank_g)
                t_done(t_mm_g)
                sch.add("scalar", (lambda eng, c=c, bank=bank_h:
                    eng.activation(out=rf[:, c * STK:(c + 1) * STK],
                                   in_=ps[bank][:], func=AF.Relu)),
                    writes=[(rf, c)], deps=[t_mm_h])
                t_mn = sch.add("scalar", (lambda eng, c=c, bank=bank_h:
                    eng.activation(out=mf[:, c * STK:(c + 1) * STK],
                                   in_=ps[bank][:], func=AF.Relu, scale=-1.0)),
                    writes=[(mf, c)], deps=[t_mm_h])
                ps_free[bank_h] = [t_mn]
                t_z = sch.add("scalar", (lambda eng, c=c, bank=bank_g:
                    eng.activation(out=zf[:, c * STK:(c + 1) * STK],
                                   in_=ps[bank][:], func=AF.Sigmoid)),
                    writes=[(zf, c)], deps=[t_mm_g])
                ps_free[bank_g] = [t_z]
                sch.add("scalar", (lambda eng, c=c: eng.activation(
                    out=mf[:, c * STK:(c + 1) * STK],
                    in_=mf[:, c * STK:(c + 1) * STK], func=AF.Sigmoid,
                    scale=-1.0)),
                    reads=[(mf, c)], writes=[(mf, c)])
                sch.add("vector", (lambda eng, c=c: eng.tensor_tensor(
                    out=rf[:, c * STK:(c + 1) * STK],
                    in0=rf[:, c * STK:(c + 1) * STK],
                    in1=mf[:, c * STK:(c + 1) * STK], op=OP.add)),
                    reads=[(rf, c), (mf, c)], writes=[(rf, c)])
                sch.add("vector", (lambda eng, c=c: eng.tensor_tensor(
                    out=rf[:, c * STK:(c + 1) * STK],
                    in0=zf[:, c * STK:(c + 1) * STK],
                    in1=rf[:, c * STK:(c + 1) * STK], op=OP.mult)),
                    reads=[(zf, c), (rf, c)], writes=[(rf, c)])
                sch.add("vector", (lambda eng, c=c: eng.tensor_scalar(
                    out=mf[:, c * STK:(c + 1) * STK],
                    in0=zf[:, c * STK:(c + 1) * STK],
                    scalar1=-1.0, scalar2=1.0, op0=OP.mult, op1=OP.add)),
                    reads=[(zf, c), (mf, c)], writes=[(mf, c)])
                init = 0.0 if first else carry[:, cb + c:cb + c + 1]
                sch.add("vector", (lambda eng, c=c, init=init:
                    eng.tensor_tensor_scan(
                        out=zf[:, c * STK:(c + 1) * STK],
                        data0=mf[:, c * STK:(c + 1) * STK],
                        data1=rf[:, c * STK:(c + 1) * STK],
                        initial=init, op0=OP.mult, op1=OP.add)),
                    reads=[(mf, c), (rf, c)] + ([] if first else [carry]),
                    writes=[(zf, c)])
            sch.add("vector", lambda eng, cb=cb: eng.tensor_copy(
                r3(carry[:, cb:cb + DC], "p (c o) -> p c o", o=1),
                r3(zf[:], "p (c s) -> p c s", c=DC)[:, :, STK - 1:STK]),
                reads=[(zf, c) for c in range(DC)], writes=[carry])

        def back_LN3(m, stg):
            ln_stats(x_tms[stg % 2])
            ln3_toks[0] = ln_apply(bfA, x_tms[stg % 2])

        def back_yT(m):
            transpose_tm_to_fm(bfA, xsf, st_deps=ln3_toks[0])

        def mk_mix(m, mu_t, dst):
            yb = (m // 4) * DC
            mu_sb = mu_t[:]
            for c in range(DC):
                def mx(eng, c=c, mu_sb=mu_sb, yb=yb, dst=dst[:]):
                    eng.scalar_tensor_tensor(
                        out=dst[:, c * STK + 1:(c + 1) * STK],
                        in0=B2[:, c * STK + 1:(c + 1) * STK],
                        scalar=mu_sb[:, c:c + 1],
                        in1=xsf[:, c * STK:(c + 1) * STK - 1],
                        op0=OP.mult, op1=OP.add)
                    return eng.scalar_tensor_tensor(
                        out=dst[:, c * STK:c * STK + 1],
                        in0=B2[:, c * STK:c * STK + 1],
                        scalar=mu_sb[:, c:c + 1],
                        in1=ylast[:, yb + c:yb + c + 1],
                        op0=OP.mult, op1=OP.add)
                sch.add("vector", mx, reads=[(B2, c), (xsf, c), ylast, mu_t],
                        writes=[dst])

        def back_mix_k(m):
            first = (m % 4 == 0)
            yb = (m // 4) * DC
            if first:
                sch.add("vector", lambda eng, yb=yb: eng.memset(
                    ylast[:, yb:yb + DC], 0.0), writes=[ylast])
            xs3 = r3(xsf[:], "p (c s) -> p c s", c=DC)
            d3 = r3(B2[:], "p (c s) -> p c s", c=DC)
            sch.add("vector", lambda eng: eng.tensor_tensor(
                out=d3[:, :, 1:STK], in0=xs3[:, :, 1:STK],
                in1=xs3[:, :, 0:STK - 1], op=OP.subtract),
                reads=[(xsf, c) for c in range(DC)], writes=b2f)
            sch.add("vector", lambda eng, yb=yb: eng.tensor_tensor(
                out=d3[:, :, 0:1], in0=xs3[:, :, 0:1],
                in1=r3(ylast[:, yb:yb + DC], "p (c o) -> p c o", o=1),
                op=OP.subtract),
                reads=[(xsf, c) for c in range(DC)] + [ylast], writes=b2f)
            mk_mix(m, muk_sb, xkb)

        def back_mix_r(m):
            yb = (m // 4) * DC
            xs3 = r3(xsf[:], "p (c s) -> p c s", c=DC)
            mk_mix(m, mur_sb, bfA)
            sch.add("vector", lambda eng, yb=yb: eng.tensor_copy(
                r3(ylast[:, yb:yb + DC], "p (c o) -> p c o", o=1),
                xs3[:, :, STK - 1:STK]),
                reads=[(xsf, c) for c in range(DC)], writes=[ylast])
            def k_evacs(oc):
                return [("scalar", (lambda eng, bank_ap, oc=oc: eng.activation(
                    out=B2[:, oc * STK:(oc + 1) * STK], in_=bank_ap,
                    func=AF.Relu)), [], [(B2, oc)])]
            mm_block(w_k, D, DC, xkb, tok_wk, k_evacs)
            def r_evacs(oc):
                return [("scalar", (lambda eng, bank_ap, oc=oc: eng.activation(
                    out=B1[:, oc * STK:(oc + 1) * STK], in_=bank_ap,
                    func=AF.Sigmoid)), [], [B1])]
            mm_block(w_r, D, DC, bfA, tok_wr, r_evacs)

        def back_ka_kv(m):
            sch.add("scalar", lambda eng: eng.activation(
                out=bfA[:], in_=B2[:], func=AF.Square), reads=b2f, writes=[bfA])
            def kv_evacs(oc):
                return [("vector", (lambda eng, bank_ap, oc=oc: eng.tensor_tensor(
                    out=B2[:, oc * STK:(oc + 1) * STK],
                    in0=B1[:, oc * STK:(oc + 1) * STK], in1=bank_ap,
                    op=OP.mult)), [B1], [(B2, oc)])]
            mm_block(w_v, D, DC, bfA, tok_wv, kv_evacs)

        def back_oT(m):
            transpose_fm_to_tm(B2, B1, fine=True)

        def back_LN4_out(m, stg):
            x_tm = x_tms[stg % 2]
            b, q = m // 4, m % 4
            ln_stats(B1)
            ln_apply(B2, B1, engine="vector")
            sch.add("vector", lambda eng, x_tm=x_tm: eng.tensor_tensor(
                out=B2[:], in0=x_tm[:], in1=B2[:], op=OP.add),
                reads=[x_tm] + b2f, writes=b2f)
            dst = r3(y_h[b, q * STK:(q + 1) * STK, :], "(st p) d -> p st d", p=P)
            sch.add("sync", lambda eng, dst=dst: eng.dma_start(
                out=dst, in_=r3(B2[:], "p (st d) -> p st d", d=D)),
                reads=b2f, sem="out_s", inc=16,
                hard=[("out_s", 16 * stg)] if stg >= 1 else [])

        morder = [0, 4, 1, 5, 2, 6, 3, 7][:NM] if NM == 8 else list(range(NM))
        x_load(morder[0], 0)
        if NM > 1:
            x_load(morder[1], 1)
        for stg in range(NM + 1):
            f = morder[stg] if stg < NM else None
            m = morder[stg - 1] if stg >= 1 else None
            if m is not None:
                back_LN2(m, stg - 1)
            if f is not None:
                front_LN1(f, stg)
                front_xsT(f)
            if m is not None:
                back_LN3(m, stg - 1)
            if f is not None:
                front_attn_scan(f)
            if m is not None:
                back_yT(m)
                back_mix_k(m)
                back_mix_r(m)
                back_ka_kv(m)
                back_oT(m)
                back_LN4_out(m, stg - 1)
            if f is not None:
                back_hT(f)
            if m is not None and stg + 1 < NM:
                x_load(morder[stg + 1], stg + 1)

        sch.add("sync", lambda eng: None, deps=[("out_s", sch.count["out_s"])],
                sem="in_s", inc=0)

        with nc.Block() as block:
            @block.sync
            def _(eng):
                sch.emit("sync", eng, sems)

            @block.tensor
            def _(eng):
                sch.emit("tensor", eng, sems)

            @block.vector
            def _(eng):
                sch.emit("vector", eng, sems)

            @block.scalar
            def _(eng):
                sch.emit("scalar", eng, sems)

            @block.gpsimd
            def _(eng):
                sch.emit("gpsimd", eng, sems)

    return nc


_NC_CACHE = None


def kernel(L0_attn_x, L0_attn_W_hg, L0_ffn_W_r, L0_ffn_W_k, L0_ffn_W_v,
           L0_ffn_mu_r, L0_ffn_mu_k):
    global _NC_CACHE
    x = np.ascontiguousarray(np.asarray(L0_attn_x, dtype=np.float32))
    whgT = np.ascontiguousarray(np.asarray(L0_attn_W_hg, dtype=np.float32).T)
    wrT = np.ascontiguousarray(np.asarray(L0_ffn_W_r, dtype=np.float32).T)
    wkT = np.ascontiguousarray(np.asarray(L0_ffn_W_k, dtype=np.float32).T)
    wvT = np.ascontiguousarray(np.asarray(L0_ffn_W_v, dtype=np.float32).T)
    mur = np.ascontiguousarray(np.asarray(L0_ffn_mu_r, dtype=np.float32))
    muk = np.ascontiguousarray(np.asarray(L0_ffn_mu_k, dtype=np.float32))

    if _NC_CACHE is None:
        _NC_CACHE = build_nc()
    nc = _NC_CACHE

    in_maps = []
    for core in range(8):
        in_maps.append({
            "x": x[core * BC:(core + 1) * BC],
            "whgT": whgT, "wrT": wrT, "wkT": wkT, "wvT": wvT,
            "mur": mur, "muk": muk,
        })
    res = run_bass_kernel_spmd(nc, in_maps, core_ids=list(range(8)))
    out = np.concatenate([r["y"] for r in res.results], axis=0)
    return out


# version 28
# speedup vs baseline: 1.4075x; 1.4075x over previous
import sys
sys.path.insert(0, "/opt/trn_rl_repo")
import contextlib
import numpy as np
import concourse.bass as bass
import concourse.mybir as mybir
from concourse.bass_utils import run_bass_kernel_spmd

F32 = mybir.dt.float32
BF16 = mybir.dt.bfloat16
AF = mybir.ActivationFunctionType
OP = mybir.AluOpType
AX = mybir.AxisListType

P = 128
D = 1024
E = 2048
S = 2048
BC = 2
NM = 8
STK = 512
NST = 4
DC = D // P
ECN = E // P
EPS = 1e-5
DBG = False
DBG_F = -1
SERIAL = False

ENGINES = ("sync", "tensor", "vector", "scalar", "gpsimd")
OWN_SEM = {"sync": "in_s", "tensor": "t_s", "vector": "v_s",
           "scalar": "s_s", "gpsimd": "g_s"}
ALL_SEMS = ("in_s", "out_s", "w_s", "g_s", "t_s", "v_s", "s_s")


class Sched:

    def __init__(self):
        self.ops = {e: [] for e in ENGINES}
        self.count = {s: 0 for s in ALL_SEMS}
        self.last_w = {}
        self.readers = {}

    @staticmethod
    def _key(buf):
        if isinstance(buf, tuple):
            return (id(buf[0]), buf[1])
        return id(buf)

    def add(self, engine, fn, reads=(), writes=(), deps=(), sem=None, inc=None,
            hard=(), war_reads=()):
        if sem is None:
            sem = OWN_SEM[engine]
        if inc is None:
            inc = 16 if engine == "sync" else 1
        deps = [d for d in deps if d is not None]
        for b in reads:
            t = self.last_w.get(self._key(b))
            if t is not None:
                deps.append(t)
        for b in writes:
            k = self._key(b)
            t = self.last_w.get(k)
            if t is not None:
                deps.append(t)
            for s, v in self.readers.get(k, {}).items():
                deps.append((s, v))
        hard = [d for d in hard if d is not None]
        self.count[sem] += inc
        tok = (sem, self.count[sem])
        self.ops[engine].append((tuple(deps), tuple(hard), fn, sem, inc))
        for b in list(reads) + list(war_reads):
            k = self._key(b)
            rd = self.readers.setdefault(k, {})
            rd[sem] = self.count[sem]
        for b in writes:
            k = self._key(b)
            if inc:
                self.last_w[k] = tok
            self.readers[k] = {}
        return tok

    def emit(self, engine, eng, sems, warm=None):
        waited = {}
        warm_ready = False
        for deps, hard, fn, sem, inc in self.ops[engine]:
            todo = {}
            for (s, v) in list(deps) + list(hard):
                if v > todo.get(s, 0):
                    todo[s] = v
            if warm is not None and todo:
                w_ap, w_tok = warm
                s_b, v_b, d_b = None, 0, 0
                for s, v in todo.items():
                    dd = v - waited.get(s, 0)
                    if dd > d_b:
                        s_b, v_b, d_b = s, v, dd
                step = 16 if s_b in ("in_s", "out_s", "w_s") else 2
                if d_b >= 3 * step:
                    if not warm_ready:
                        wi_s, wi_v = w_tok
                        if wi_v > waited.get(wi_s, 0):
                            eng.wait_ge(sems[wi_s], wi_v)
                            waited[wi_s] = wi_v
                        warm_ready = True
                    nb = min(8, d_b // step - 1)
                    for j in range(nb, 0, -1):
                        wv = v_b - j * step
                        if wv <= waited.get(s_b, 0):
                            continue
                        eng.wait_ge(sems[s_b], wv)
                        waited[s_b] = wv
                        eng.ldweights(w_ap)
            for s, v in sorted(todo.items()):
                if v <= waited.get(s, 0):
                    continue
                eng.wait_ge(sems[s], v)
                waited[s] = v
            inst = fn(eng)
            if inc and inst is not None:
                inst.then_inc(sems[sem], inc)


def r3(ap, pattern, **kw):
    return ap.rearrange(pattern, **kw)


def build_nc():
    nc = bass.Bass("TRN2", target_bir_lowering=False,
                   dynamic_dma_scratch_size=6144)
    x_h = nc.dram_tensor("x", [BC, S, D], F32, kind="ExternalInput").ap()
    whg_h = nc.dram_tensor("whgT", [D, E], F32, kind="ExternalInput").ap()
    wr_h = nc.dram_tensor("wrT", [D, D], F32, kind="ExternalInput").ap()
    wk_h = nc.dram_tensor("wkT", [D, D], F32, kind="ExternalInput").ap()
    wv_h = nc.dram_tensor("wvT", [D, D], F32, kind="ExternalInput").ap()
    mur_h = nc.dram_tensor("mur", [D], F32, kind="ExternalInput").ap()
    muk_h = nc.dram_tensor("muk", [D], F32, kind="ExternalInput").ap()
    y_h = nc.dram_tensor("y", [BC, S, D], F32, kind="ExternalOutput").ap()

    with contextlib.ExitStack() as es:
        sb = lambda name, shape, dt: es.enter_context(nc.sbuf_tensor(name, shape, dt))

        w_hg = sb("w_hg", [P, DC * E], BF16)
        w_r = sb("w_r", [P, DC * D], BF16)
        w_k = sb("w_k", [P, DC * D], BF16)
        w_v = sb("w_v", [P, DC * D], BF16)
        mur_sb = sb("mur_sb", [P, DC], F32)
        muk_sb = sb("muk_sb", [P, DC], F32)
        id32 = sb("id32", [P, P], F32)
        id16 = sb("id16", [P, P], BF16)
        x_tms = [sb("x_tm0", [P, NST * D], F32), sb("x_tm1", [P, NST * D], F32)]
        bfA = sb("bfA", [P, 4096], BF16)
        xsf = sb("xsf", [P, 4096], BF16)
        zf = sb("zf", [P, 4096], F32)
        rf = sb("rf", [P, 4096], F32)
        mf = sb("mf", [P, 4096], F32)
        B1 = sb("B1", [P, 4096], F32)
        B2 = sb("B2", [P, 4096], F32)
        xkb = sb("xkb", [P, 4096], BF16)
        stats = sb("stats", [P, 48], F32)
        red1 = sb("red1", [P, 12], F32)
        sqscr = sb("sqscr", [P, 16], F32)
        negm = sb("negm", [P, 4], F32)
        rstd = sb("rstd", [P, 4], F32)
        msq = sb("msq", [P, 4], F32)
        t1v = sb("t1v", [P, 4], F32)
        varv = sb("varv", [P, 4], F32)
        nmr = sb("nmr", [P, 4], F32)
        carry = sb("carry", [P, 2 * DC], F32)
        ylast = sb("ylast", [P, 2 * DC], BF16)

        ps = [es.enter_context(nc.psum_tensor(f"ps{i}", [P, 512], F32))
              for i in range(4)]
        pt = [es.enter_context(nc.psum_tensor(f"pt{i}", [P, 512], F32))
              for i in range(2)]
        pb = [es.enter_context(nc.psum_tensor(f"pb{i}", [P, 512], BF16))
              for i in range(2)]

        sems = {n: es.enter_context(nc.semaphore(n)) for n in ALL_SEMS}

        sch = Sched()
        b2f = [(B2, c) for c in range(DC)]

        def pe_warm(extra_deps=()):
            def dmm(eng):
                return eng.matmul(pt[0][:, 0:64], id16[:], id16[:, 0:64],
                                  start=True, stop=True)
            sch.add("tensor", dmm, reads=[id16],
                    deps=list(extra_deps) + [pt_free[0]])

        sch.add("gpsimd", lambda eng: eng.memset(id32[:], 0.0), writes=[id32])
        sch.add("gpsimd", lambda eng: eng.affine_select(
            out=id32[:], in_=id32[:], compare_op=OP.not_equal,
            fill=1.0, base=0, pattern=[[-1, P]], channel_multiplier=1),
            reads=[id32], writes=[id32])
        sch.add("gpsimd", lambda eng: eng.memset(id16[:], 0.0), writes=[id16])
        sch.add("gpsimd", lambda eng: eng.affine_select(
            out=id16[:], in_=id16[:], compare_op=OP.not_equal,
            fill=1.0, base=0, pattern=[[-1, P]], channel_multiplier=1),
            reads=[id16], writes=[id16])
        id16_tok = sch.last_w[Sched._key(id16)]

        def dma_w(dst, src_ap):
            return lambda eng: eng.dma_start(out=dst, in_=src_ap)
        tok_whg = sch.add("gpsimd", dma_w(r3(w_hg[:], "p (dc e) -> p dc e", dc=DC),
                                          r3(whg_h, "(dc p) e -> p dc e", p=P)),
                          writes=[w_hg], sem="w_s", inc=16)
        tok_wr = sch.add("gpsimd", dma_w(r3(w_r[:], "p (dc o) -> p dc o", dc=DC),
                                         r3(wr_h, "(dc p) o -> p dc o", p=P)),
                         writes=[w_r], sem="w_s", inc=16, hard=[tok_whg])
        tok_wk = sch.add("gpsimd", dma_w(r3(w_k[:], "p (dc o) -> p dc o", dc=DC),
                                         r3(wk_h, "(dc p) o -> p dc o", p=P)),
                         writes=[w_k], sem="w_s", inc=16, hard=[tok_wr])
        tok_wv = sch.add("gpsimd", dma_w(r3(w_v[:], "p (dc o) -> p dc o", dc=DC),
                                         r3(wv_h, "(dc p) o -> p dc o", p=P)),
                         writes=[w_v], sem="w_s", inc=16, hard=[tok_wk])

        def dma_mu(dst, src_ap):
            return lambda eng: eng.dma_start(out=dst, in_=src_ap,
                                             allow_slow_non_contiguous=True)
        tok_mur = sch.add("gpsimd", dma_mu(mur_sb[:], r3(mur_h, "(c p) -> p c", p=P)),
                          writes=[mur_sb], sem="w_s", inc=16, hard=[tok_wv])
        sch.add("gpsimd", dma_mu(muk_sb[:], r3(muk_h, "(c p) -> p c", p=P)),
                writes=[muk_sb], sem="w_s", inc=16, hard=[tok_mur])

        ps_free = [None, None, None, None]
        pt_free = [None, None]
        pb_free = [None, None]
        t_kind = [None, None]

        def t_switch(kind):
            prev_kind, prev_tok = t_kind
            t_kind[0] = kind
            if prev_kind is not None and prev_kind != kind:
                return [prev_tok]
            return []

        def t_done(tok):
            t_kind[1] = tok

        def x_load(m, stg):
            b, q = m // 4, m % 4
            src = r3(x_h[b, q * STK:(q + 1) * STK, :], "(st p) d -> p st d", p=P)
            dst = r3(x_tms[stg % 2][:], "p (st d) -> p st d", d=D)
            sch.add("sync", lambda eng, s=src, d=dst: eng.dma_start(out=d, in_=s),
                    writes=[x_tms[stg % 2]], sem="in_s", inc=16,
                    hard=[("in_s", 16 * stg)] if stg >= 1 else [])

        def ln_stats(src_t):
            src = src_t[:]
            def bn(eng):
                last = None
                for st in range(NST):
                    for hlf in range(2):
                        last = eng.bn_stats(
                            stats[:, st * 12 + hlf * 6: st * 12 + hlf * 6 + 6],
                            src[:, st * D + hlf * 512: st * D + (hlf + 1) * 512])
                return last
            t_bn0 = sch.add("vector", bn, reads=[src_t], writes=[stats])
            pe_warm([t_bn0])

            sv = r3(stats[:], "p (st g t) -> p st t g", st=4, g=4)
            means16 = r3(stats[:], "p (sg t) -> p sg t", t=3)[:, :, 1:2]
            rmean = r3(red1[:], "p (st t) -> p t st", t=3)
            sch.add("vector", lambda eng: eng.tensor_reduce(
                r3(red1[:], "p (st t o) -> p st t o", st=4, o=1), sv,
                axis=AX.X, op=OP.add), reads=[stats], writes=[red1])
            sch.add("vector", lambda eng: eng.tensor_tensor(
                out=r3(sqscr[:], "p (a o) -> p a o", o=1),
                in0=means16, in1=means16, op=OP.mult),
                reads=[stats], writes=[sqscr])
            sch.add("vector", lambda eng: eng.tensor_reduce(
                r3(t1v[:], "p (a o) -> p a o", o=1),
                r3(sqscr[:], "p (st g) -> p st g", g=4),
                axis=AX.X, op=OP.add), reads=[sqscr], writes=[t1v])
            sch.add("vector", lambda eng: eng.tensor_scalar(
                out=r3(negm[:], "p (o st) -> p o st", o=1),
                in0=rmean[:, 1:2, :], scalar1=-0.25, scalar2=None,
                op0=OP.mult), reads=[red1], writes=[negm])
            sch.add("vector", lambda eng: eng.tensor_tensor(
                out=msq[:], in0=negm[:], in1=negm[:], op=OP.mult),
                reads=[negm], writes=[msq])
            sch.add("vector", lambda eng: eng.tensor_scalar(
                out=sqscr[:, 0:4], in0=msq[:], scalar1=-EPS,
                scalar2=None, op0=OP.add), reads=[msq], writes=[sqscr])
            sch.add("vector", lambda eng: eng.scalar_tensor_tensor(
                out=t1v[:], in0=t1v[:], scalar=0.25, in1=sqscr[:, 0:4],
                op0=OP.mult, op1=OP.subtract),
                reads=[t1v, sqscr], writes=[t1v])
            sch.add("vector", lambda eng: eng.scalar_tensor_tensor(
                out=r3(varv[:], "p (o st) -> p o st", o=1),
                in0=rmean[:, 2:3, :], scalar=1.0 / 1024.0,
                in1=r3(t1v[:], "p (o st) -> p o st", o=1),
                op0=OP.mult, op1=OP.add),
                reads=[red1, t1v], writes=[varv])
            sch.add("scalar", lambda eng: eng.activation(
                out=t1v[:], in_=varv[:], func=AF.Ln),
                reads=[varv], writes=[t1v])
            t_rs = sch.add("scalar", lambda eng: eng.activation(
                out=rstd[:], in_=t1v[:], func=AF.Exp, scale=-0.5),
                reads=[t1v], writes=[rstd])
            pe_warm([t_rs])


        def ln_apply(dst_t, src_t, engine="vector"):
            dst, src = dst_t[:], src_t[:]
            dkeys = b2f if dst_t is B2 else [dst_t]
            skeys = b2f if src_t is B2 else [src_t]
            toks = []
            for st in range(NST):
                if engine == "scalar":
                    def apl(eng, st=st):
                        return eng.activation(
                            out=dst[:, st * D:(st + 1) * D],
                            in_=src[:, st * D:(st + 1) * D],
                            func=AF.Identity,
                            scale=rstd[:, st:st + 1], bias=nmr[:, st:st + 1])
                    toks.append(sch.add("scalar", apl,
                                        reads=skeys + [rstd, nmr], writes=dkeys))
                else:
                    def apl(eng, st=st):
                        return eng.tensor_scalar(
                            out=dst[:, st * D:(st + 1) * D],
                            in0=src[:, st * D:(st + 1) * D],
                            scalar1=negm[:, st:st + 1], scalar2=rstd[:, st:st + 1],
                            op0=OP.add, op1=OP.mult)
                    toks.append(sch.add("vector", apl,
                                        reads=skeys + [negm, rstd], writes=dkeys))
            return toks

        def transpose_tm_to_fm(src_t, dst_t, st_deps=None):
            nonlocal pb_free
            src, dst = src_t[:], dst_t[:]
            for st in range(NST):
                for half in range(2):
                    bank = half
                    def tr(eng, st=st, half=half, bank=bank, src=src):
                        last = None
                        for ci in range(4):
                            c = half * 4 + ci
                            last = eng.transpose(
                                pb[bank][:, ci * P:(ci + 1) * P],
                                src[:, st * D + c * P: st * D + (c + 1) * P],
                                id16[:])
                        return last
                    dep = [st_deps[st]] if st_deps else []
                    rd = [id16] if st_deps else [src_t, id16]
                    t_tr = sch.add("tensor", tr, reads=rd,
                                   war_reads=[src_t] if st_deps else [],
                                   deps=dep + [pb_free[bank]],
                                   hard=t_switch("tr"))
                    t_done(t_tr)
                    def ev(eng, st=st, half=half, bank=bank, dst=dst):
                        dv = r3(dst, "p (c s) -> p c s", c=DC)
                        return eng.copy(
                            out=dv[:, half * 4:(half + 1) * 4,
                                   st * P:(st + 1) * P],
                            in_=r3(pb[bank][:], "p (ci s) -> p ci s", ci=4))
                    t_ev = sch.add("scalar", ev,
                                   writes=[(dst_t, c)
                                           for c in range(half * 4, half * 4 + 4)],
                                   deps=[t_tr])
                    pb_free[bank] = t_ev

        def transpose_fm_to_tm(src_t, dst_t, fine=False):
            nonlocal pt_free
            src, dst = src_t[:], dst_t[:]
            for st in range(NST):
                for half in range(2):
                    bank = half
                    def tr(eng, st=st, half=half, bank=bank, src=src):
                        last = None
                        for ci in range(4):
                            c = half * 4 + ci
                            last = eng.transpose(
                                pt[bank][:, ci * P:(ci + 1) * P],
                                src[:, c * STK + st * P: c * STK + (st + 1) * P],
                                id32[:])
                        return last
                    rd = ([(src_t, c) for c in range(half * 4, half * 4 + 4)]
                          if fine else [src_t])
                    t_tr = sch.add("tensor", tr, reads=rd + [id32],
                                   deps=[pt_free[bank]], hard=t_switch("tr"))
                    t_done(t_tr)
                    def ev(eng, st=st, half=half, bank=bank, dst=dst):
                        return eng.copy(
                            out=dst[:, st * D + half * 512: st * D + (half + 1) * 512],
                            in_=pt[bank][:])
                    t_ev = sch.add("scalar", ev, writes=[dst_t], deps=[t_tr])
                    pt_free[bank] = t_ev

        def mm_block(w_t, w_stride, n_out, rhs_t, w_tok, evac_fns):
            nonlocal ps_free
            w_sb, rhs = w_t[:], rhs_t[:]
            for oc in range(n_out):
                bank = oc % 4
                def mm(eng, oc=oc, w_sb=w_sb, w_stride=w_stride, rhs=rhs, bank=bank):
                    last = None
                    for dc in range(DC):
                        last = eng.matmul(
                            ps[bank][:],
                            w_sb[:, dc * w_stride + oc * P: dc * w_stride + (oc + 1) * P],
                            rhs[:, dc * STK:(dc + 1) * STK],
                            start=(dc == 0), stop=(dc == DC - 1))
                    return last
                bank_deps = ps_free[bank] or []
                t_mm = sch.add("tensor", mm, reads=[w_t, rhs_t],
                               deps=[w_tok] + list(bank_deps),
                               hard=t_switch("mm"))
                t_done(t_mm)
                prev_ev = None
                for (engine, fn, rd, wr) in evac_fns(oc):
                    prev_ev = sch.add(
                        engine, (lambda eng, fn=fn, bank=bank: fn(eng, ps[bank][:])),
                        reads=list(rd), writes=list(wr),
                        deps=[t_mm] + ([prev_ev] if prev_ev else []))
                ps_free[bank] = [prev_ev]

        dbg_toks = {}
        def dbg_dump(name, buf, dt=F32):
            if not DBG:
                return
            t = nc.dram_tensor(name, [P, 4096], dt, kind="ExternalOutput").ap()
            sch.add("sync", lambda eng, t=t, buf=buf: eng.dma_start(out=t, in_=buf[:]),
                    reads=[buf], sem="out_s", inc=16)
            dbg_toks[name] = True


        ln1_toks = [None]
        ln3_toks = [None]

        def front_LN1(f, stg):
            ln_stats(x_tms[stg % 2])
            ln1_toks[0] = ln_apply(bfA, x_tms[stg % 2])

        def back_hT(m):
            transpose_fm_to_tm(zf, B1, fine=True)

        def front_xsT(f):
            transpose_tm_to_fm(bfA, xsf, st_deps=ln1_toks[0])

        def back_LN2(m, stg):
            x_tm = x_tms[stg % 2]
            ln_stats(B1)
            ln_apply(B2, B1, engine="vector")
            sch.add("vector", lambda eng, x_tm=x_tm: eng.tensor_tensor(
                out=x_tm[:], in0=x_tm[:], in1=B2[:], op=OP.add),
                reads=[x_tm] + b2f, writes=[x_tm])

        def front_attn_scan(f):
            nonlocal ps_free
            first = (f % 4 == 0)
            cb = (f // 4) * DC
            w_sb, rhs = w_hg[:], xsf[:]

            def mm_group(ec, bank):
                def mm(eng, ec=ec, bank=bank):
                    last = None
                    for dc in range(DC):
                        last = eng.matmul(
                            ps[bank][:],
                            w_sb[:, dc * E + ec * P: dc * E + (ec + 1) * P],
                            rhs[:, dc * STK:(dc + 1) * STK],
                            start=(dc == 0), stop=(dc == DC - 1))
                    return last
                bank_deps = ps_free[bank] or []
                return sch.add("tensor", mm,
                               reads=[w_hg] + [(xsf, cc) for cc in range(DC)],
                               deps=[tok_whg] + list(bank_deps),
                               hard=t_switch("mm"))

            for c in range(DC):
                bank_h = (2 * c) % 4
                bank_g = (2 * c + 1) % 4
                t_mm_h = mm_group(c, bank_h)
                t_done(t_mm_h)
                t_mm_g = mm_group(DC + c, bank_g)
                t_done(t_mm_g)
                sch.add("scalar", (lambda eng, c=c, bank=bank_h:
                    eng.activation(out=rf[:, c * STK:(c + 1) * STK],
                                   in_=ps[bank][:], func=AF.Relu)),
                    writes=[(rf, c)], deps=[t_mm_h])
                t_mn = sch.add("scalar", (lambda eng, c=c, bank=bank_h:
                    eng.activation(out=mf[:, c * STK:(c + 1) * STK],
                                   in_=ps[bank][:], func=AF.Relu, scale=-1.0)),
                    writes=[(mf, c)], deps=[t_mm_h])
                ps_free[bank_h] = [t_mn]
                t_z = sch.add("scalar", (lambda eng, c=c, bank=bank_g:
                    eng.activation(out=zf[:, c * STK:(c + 1) * STK],
                                   in_=ps[bank][:], func=AF.Sigmoid)),
                    writes=[(zf, c)], deps=[t_mm_g])
                ps_free[bank_g] = [t_z]
                sch.add("scalar", (lambda eng, c=c: eng.activation(
                    out=mf[:, c * STK:(c + 1) * STK],
                    in_=mf[:, c * STK:(c + 1) * STK], func=AF.Sigmoid,
                    scale=-1.0)),
                    reads=[(mf, c)], writes=[(mf, c)])
                sch.add("vector", (lambda eng, c=c: eng.tensor_tensor(
                    out=rf[:, c * STK:(c + 1) * STK],
                    in0=rf[:, c * STK:(c + 1) * STK],
                    in1=mf[:, c * STK:(c + 1) * STK], op=OP.add)),
                    reads=[(rf, c), (mf, c)], writes=[(rf, c)])
                sch.add("vector", (lambda eng, c=c: eng.tensor_tensor(
                    out=rf[:, c * STK:(c + 1) * STK],
                    in0=zf[:, c * STK:(c + 1) * STK],
                    in1=rf[:, c * STK:(c + 1) * STK], op=OP.mult)),
                    reads=[(zf, c), (rf, c)], writes=[(rf, c)])
                sch.add("vector", (lambda eng, c=c: eng.tensor_scalar(
                    out=mf[:, c * STK:(c + 1) * STK],
                    in0=zf[:, c * STK:(c + 1) * STK],
                    scalar1=-1.0, scalar2=1.0, op0=OP.mult, op1=OP.add)),
                    reads=[(zf, c), (mf, c)], writes=[(mf, c)])
                init = 0.0 if first else carry[:, cb + c:cb + c + 1]
                sch.add("vector", (lambda eng, c=c, init=init:
                    eng.tensor_tensor_scan(
                        out=zf[:, c * STK:(c + 1) * STK],
                        data0=mf[:, c * STK:(c + 1) * STK],
                        data1=rf[:, c * STK:(c + 1) * STK],
                        initial=init, op0=OP.mult, op1=OP.add)),
                    reads=[(mf, c), (rf, c)] + ([] if first else [carry]),
                    writes=[(zf, c)])
            sch.add("vector", lambda eng, cb=cb: eng.tensor_copy(
                r3(carry[:, cb:cb + DC], "p (c o) -> p c o", o=1),
                r3(zf[:], "p (c s) -> p c s", c=DC)[:, :, STK - 1:STK]),
                reads=[(zf, c) for c in range(DC)], writes=[carry])

        def back_LN3(m, stg):
            ln_stats(x_tms[stg % 2])
            ln3_toks[0] = ln_apply(bfA, x_tms[stg % 2])

        def back_yT(m):
            transpose_tm_to_fm(bfA, xsf, st_deps=ln3_toks[0])

        def mk_mix(m, mu_t, dst):
            yb = (m // 4) * DC
            mu_sb = mu_t[:]
            for c in range(DC):
                def mx(eng, c=c, mu_sb=mu_sb, yb=yb, dst=dst[:]):
                    eng.scalar_tensor_tensor(
                        out=dst[:, c * STK + 1:(c + 1) * STK],
                        in0=B2[:, c * STK + 1:(c + 1) * STK],
                        scalar=mu_sb[:, c:c + 1],
                        in1=xsf[:, c * STK:(c + 1) * STK - 1],
                        op0=OP.mult, op1=OP.add)
                    return eng.scalar_tensor_tensor(
                        out=dst[:, c * STK:c * STK + 1],
                        in0=B2[:, c * STK:c * STK + 1],
                        scalar=mu_sb[:, c:c + 1],
                        in1=ylast[:, yb + c:yb + c + 1],
                        op0=OP.mult, op1=OP.add)
                sch.add("vector", mx, reads=[(B2, c), (xsf, c), ylast, mu_t],
                        writes=[dst])

        def back_mix_k(m):
            first = (m % 4 == 0)
            yb = (m // 4) * DC
            if first:
                sch.add("vector", lambda eng, yb=yb: eng.memset(
                    ylast[:, yb:yb + DC], 0.0), writes=[ylast])
            xs3 = r3(xsf[:], "p (c s) -> p c s", c=DC)
            d3 = r3(B2[:], "p (c s) -> p c s", c=DC)
            sch.add("vector", lambda eng: eng.tensor_tensor(
                out=d3[:, :, 1:STK], in0=xs3[:, :, 1:STK],
                in1=xs3[:, :, 0:STK - 1], op=OP.subtract),
                reads=[(xsf, c) for c in range(DC)], writes=b2f)
            sch.add("vector", lambda eng, yb=yb: eng.tensor_tensor(
                out=d3[:, :, 0:1], in0=xs3[:, :, 0:1],
                in1=r3(ylast[:, yb:yb + DC], "p (c o) -> p c o", o=1),
                op=OP.subtract),
                reads=[(xsf, c) for c in range(DC)] + [ylast], writes=b2f)
            mk_mix(m, muk_sb, xkb)

        def back_mix_r(m):
            yb = (m // 4) * DC
            xs3 = r3(xsf[:], "p (c s) -> p c s", c=DC)
            mk_mix(m, mur_sb, bfA)
            sch.add("vector", lambda eng, yb=yb: eng.tensor_copy(
                r3(ylast[:, yb:yb + DC], "p (c o) -> p c o", o=1),
                xs3[:, :, STK - 1:STK]),
                reads=[(xsf, c) for c in range(DC)], writes=[ylast])
            def k_evacs(oc):
                return [("scalar", (lambda eng, bank_ap, oc=oc: eng.activation(
                    out=B2[:, oc * STK:(oc + 1) * STK], in_=bank_ap,
                    func=AF.Relu)), [], [(B2, oc)])]
            mm_block(w_k, D, DC, xkb, tok_wk, k_evacs)
            def r_evacs(oc):
                return [("scalar", (lambda eng, bank_ap, oc=oc: eng.activation(
                    out=B1[:, oc * STK:(oc + 1) * STK], in_=bank_ap,
                    func=AF.Sigmoid)), [], [B1])]
            mm_block(w_r, D, DC, bfA, tok_wr, r_evacs)

        def back_ka_kv(m):
            sch.add("scalar", lambda eng: eng.activation(
                out=bfA[:], in_=B2[:], func=AF.Square), reads=b2f, writes=[bfA])
            def kv_evacs(oc):
                return [("vector", (lambda eng, bank_ap, oc=oc: eng.tensor_tensor(
                    out=B2[:, oc * STK:(oc + 1) * STK],
                    in0=B1[:, oc * STK:(oc + 1) * STK], in1=bank_ap,
                    op=OP.mult)), [B1], [(B2, oc)])]
            mm_block(w_v, D, DC, bfA, tok_wv, kv_evacs)

        def back_oT(m):
            transpose_fm_to_tm(B2, B1, fine=True)

        def back_LN4_out(m, stg):
            x_tm = x_tms[stg % 2]
            b, q = m // 4, m % 4
            ln_stats(B1)
            ln_apply(B2, B1, engine="vector")
            sch.add("vector", lambda eng, x_tm=x_tm: eng.tensor_tensor(
                out=B2[:], in0=x_tm[:], in1=B2[:], op=OP.add),
                reads=[x_tm] + b2f, writes=b2f)
            dst = r3(y_h[b, q * STK:(q + 1) * STK, :], "(st p) d -> p st d", p=P)
            sch.add("sync", lambda eng, dst=dst: eng.dma_start(
                out=dst, in_=r3(B2[:], "p (st d) -> p st d", d=D)),
                reads=b2f, sem="out_s", inc=16,
                hard=[("out_s", 16 * stg)] if stg >= 1 else [])

        morder = [0, 4, 1, 5, 2, 6, 3, 7][:NM] if NM == 8 else list(range(NM))
        x_load(morder[0], 0)
        if NM > 1:
            x_load(morder[1], 1)
        for stg in range(NM + 1):
            f = morder[stg] if stg < NM else None
            m = morder[stg - 1] if stg >= 1 else None
            if m is not None:
                back_LN2(m, stg - 1)
            if f is not None:
                front_LN1(f, stg)
                front_xsT(f)
            if m is not None:
                back_LN3(m, stg - 1)
            if f is not None:
                front_attn_scan(f)
            if m is not None:
                back_yT(m)
                back_mix_k(m)
                back_mix_r(m)
                back_ka_kv(m)
                back_oT(m)
                back_LN4_out(m, stg - 1)
            if f is not None:
                back_hT(f)
            if m is not None and stg + 1 < NM:
                x_load(morder[stg + 1], stg + 1)

        sch.add("sync", lambda eng: None, deps=[("out_s", sch.count["out_s"])],
                sem="in_s", inc=0)

        with nc.Block() as block:
            @block.sync
            def _(eng):
                sch.emit("sync", eng, sems)

            @block.tensor
            def _(eng):
                sch.emit("tensor", eng, sems, warm=(id16[:], id16_tok))

            @block.vector
            def _(eng):
                sch.emit("vector", eng, sems)

            @block.scalar
            def _(eng):
                sch.emit("scalar", eng, sems)

            @block.gpsimd
            def _(eng):
                sch.emit("gpsimd", eng, sems)

    return nc


_NC_CACHE = None


def kernel(L0_attn_x, L0_attn_W_hg, L0_ffn_W_r, L0_ffn_W_k, L0_ffn_W_v,
           L0_ffn_mu_r, L0_ffn_mu_k):
    global _NC_CACHE
    x = np.ascontiguousarray(np.asarray(L0_attn_x, dtype=np.float32))
    whgT = np.ascontiguousarray(np.asarray(L0_attn_W_hg, dtype=np.float32).T)
    wrT = np.ascontiguousarray(np.asarray(L0_ffn_W_r, dtype=np.float32).T)
    wkT = np.ascontiguousarray(np.asarray(L0_ffn_W_k, dtype=np.float32).T)
    wvT = np.ascontiguousarray(np.asarray(L0_ffn_W_v, dtype=np.float32).T)
    mur = np.ascontiguousarray(np.asarray(L0_ffn_mu_r, dtype=np.float32))
    muk = np.ascontiguousarray(np.asarray(L0_ffn_mu_k, dtype=np.float32))

    if _NC_CACHE is None:
        _NC_CACHE = build_nc()
    nc = _NC_CACHE

    in_maps = []
    for core in range(8):
        in_maps.append({
            "x": x[core * BC:(core + 1) * BC],
            "whgT": whgT, "wrT": wrT, "wkT": wkT, "wvT": wvT,
            "mur": mur, "muk": muk,
        })
    res = run_bass_kernel_spmd(nc, in_maps, core_ids=list(range(8)))
    out = np.concatenate([r["y"] for r in res.results], axis=0)
    return out


# version 29
# speedup vs baseline: 1.4806x; 1.0519x over previous
import sys
sys.path.insert(0, "/opt/trn_rl_repo")
import contextlib
import numpy as np
import concourse.bass as bass
import concourse.mybir as mybir
from concourse.bass_utils import run_bass_kernel_spmd

F32 = mybir.dt.float32
BF16 = mybir.dt.bfloat16
AF = mybir.ActivationFunctionType
OP = mybir.AluOpType
AX = mybir.AxisListType

P = 128
D = 1024
E = 2048
S = 2048
BC = 2
NM = 8
STK = 512
NST = 4
DC = D // P
ECN = E // P
EPS = 1e-5
DBG = False
DBG_F = -1
SERIAL = False

ENGINES = ("sync", "tensor", "vector", "scalar", "gpsimd")
OWN_SEM = {"sync": "in_s", "tensor": "t_s", "vector": "v_s",
           "scalar": "s_s", "gpsimd": "g_s"}
ALL_SEMS = ("in_s", "out_s", "w_s", "g_s", "t_s", "v_s", "s_s")


class Sched:

    def __init__(self):
        self.ops = {e: [] for e in ENGINES}
        self.count = {s: 0 for s in ALL_SEMS}
        self.last_w = {}
        self.readers = {}

    @staticmethod
    def _key(buf):
        if isinstance(buf, tuple):
            return (id(buf[0]), buf[1])
        return id(buf)

    def add(self, engine, fn, reads=(), writes=(), deps=(), sem=None, inc=None,
            hard=(), war_reads=()):
        if sem is None:
            sem = OWN_SEM[engine]
        if inc is None:
            inc = 16 if engine == "sync" else 1
        deps = [d for d in deps if d is not None]
        for b in reads:
            t = self.last_w.get(self._key(b))
            if t is not None:
                deps.append(t)
        for b in writes:
            k = self._key(b)
            t = self.last_w.get(k)
            if t is not None:
                deps.append(t)
            for s, v in self.readers.get(k, {}).items():
                deps.append((s, v))
        hard = [d for d in hard if d is not None]
        self.count[sem] += inc
        tok = (sem, self.count[sem])
        self.ops[engine].append((tuple(deps), tuple(hard), fn, sem, inc))
        for b in list(reads) + list(war_reads):
            k = self._key(b)
            rd = self.readers.setdefault(k, {})
            rd[sem] = self.count[sem]
        for b in writes:
            k = self._key(b)
            if inc:
                self.last_w[k] = tok
            self.readers[k] = {}
        return tok

    def emit(self, engine, eng, sems, warm=None):
        waited = {}
        warm_ready = False
        for deps, hard, fn, sem, inc in self.ops[engine]:
            todo = {}
            for (s, v) in list(deps) + list(hard):
                if v > todo.get(s, 0):
                    todo[s] = v
            if warm is not None and todo:
                w_ap, w_tok = warm
                s_b, v_b, d_b = None, 0, 0
                for s, v in todo.items():
                    dd = v - waited.get(s, 0)
                    if dd > d_b:
                        s_b, v_b, d_b = s, v, dd
                step = 16 if s_b in ("in_s", "out_s", "w_s") else 2
                if d_b >= 3 * step:
                    if not warm_ready:
                        wi_s, wi_v = w_tok
                        if wi_v > waited.get(wi_s, 0):
                            eng.wait_ge(sems[wi_s], wi_v)
                            waited[wi_s] = wi_v
                        warm_ready = True
                    nb = min(8, d_b // step - 1)
                    for j in range(nb, 0, -1):
                        wv = v_b - j * step
                        if wv <= waited.get(s_b, 0):
                            continue
                        eng.wait_ge(sems[s_b], wv)
                        waited[s_b] = wv
                        eng.ldweights(w_ap)
            for s, v in sorted(todo.items()):
                if v <= waited.get(s, 0):
                    continue
                eng.wait_ge(sems[s], v)
                waited[s] = v
            inst = fn(eng)
            if inc and inst is not None:
                inst.then_inc(sems[sem], inc)


def r3(ap, pattern, **kw):
    return ap.rearrange(pattern, **kw)


def build_nc():
    nc = bass.Bass("TRN2", target_bir_lowering=False,
                   dynamic_dma_scratch_size=6144)
    x_h = nc.dram_tensor("x", [BC, S, D], F32, kind="ExternalInput").ap()
    whg_h = nc.dram_tensor("whgT", [D, E], F32, kind="ExternalInput").ap()
    wr_h = nc.dram_tensor("wrT", [D, D], F32, kind="ExternalInput").ap()
    wk_h = nc.dram_tensor("wkT", [D, D], F32, kind="ExternalInput").ap()
    wv_h = nc.dram_tensor("wvT", [D, D], F32, kind="ExternalInput").ap()
    mur_h = nc.dram_tensor("mur", [D], F32, kind="ExternalInput").ap()
    muk_h = nc.dram_tensor("muk", [D], F32, kind="ExternalInput").ap()
    y_h = nc.dram_tensor("y", [BC, S, D], F32, kind="ExternalOutput").ap()

    with contextlib.ExitStack() as es:
        sb = lambda name, shape, dt: es.enter_context(nc.sbuf_tensor(name, shape, dt))

        w_hg = sb("w_hg", [P, DC * E], BF16)
        w_r = sb("w_r", [P, DC * D], BF16)
        w_k = sb("w_k", [P, DC * D], BF16)
        w_v = sb("w_v", [P, DC * D], BF16)
        mur_sb = sb("mur_sb", [P, DC], F32)
        muk_sb = sb("muk_sb", [P, DC], F32)
        id32 = sb("id32", [P, P], F32)
        id16 = sb("id16", [P, P], BF16)
        x_tms = [sb("x_tm0", [P, NST * D], F32), sb("x_tm1", [P, NST * D], F32)]
        bfA = sb("bfA", [P, 4096], BF16)
        xsf = sb("xsf", [P, 4096], BF16)
        zf = sb("zf", [P, 4096], F32)
        rf = sb("rf", [P, 4096], F32)
        mf = sb("mf", [P, 4096], F32)
        B1 = sb("B1", [P, 4096], F32)
        B2 = sb("B2", [P, 4096], F32)
        xkb = sb("xkb", [P, 4096], BF16)
        stats = sb("stats", [P, 48], F32)
        red1 = sb("red1", [P, 12], F32)
        sqscr = sb("sqscr", [P, 16], F32)
        negm = sb("negm", [P, 4], F32)
        rstd = sb("rstd", [P, 4], F32)
        msq = sb("msq", [P, 4], F32)
        t1v = sb("t1v", [P, 4], F32)
        varv = sb("varv", [P, 4], F32)
        nmr = sb("nmr", [P, 4], F32)
        carry = sb("carry", [P, 2 * DC], F32)
        ylast = sb("ylast", [P, 2 * DC], BF16)

        ps = [es.enter_context(nc.psum_tensor(f"ps{i}", [P, 512], F32))
              for i in range(4)]
        pt = [es.enter_context(nc.psum_tensor(f"pt{i}", [P, 512], F32))
              for i in range(2)]
        pb = [es.enter_context(nc.psum_tensor(f"pb{i}", [P, 512], BF16))
              for i in range(2)]

        sems = {n: es.enter_context(nc.semaphore(n)) for n in ALL_SEMS}

        sch = Sched()
        b2f = [(B2, c) for c in range(DC)]

        def pe_warm(extra_deps=()):
            def dmm(eng):
                return eng.matmul(pt[0][:, 0:64], id16[:], id16[:, 0:64],
                                  start=True, stop=True)
            sch.add("tensor", dmm, reads=[id16],
                    deps=list(extra_deps) + [pt_free[0]])

        sch.add("gpsimd", lambda eng: eng.memset(id32[:], 0.0), writes=[id32])
        sch.add("gpsimd", lambda eng: eng.affine_select(
            out=id32[:], in_=id32[:], compare_op=OP.not_equal,
            fill=1.0, base=0, pattern=[[-1, P]], channel_multiplier=1),
            reads=[id32], writes=[id32])
        sch.add("gpsimd", lambda eng: eng.memset(id16[:], 0.0), writes=[id16])
        sch.add("gpsimd", lambda eng: eng.affine_select(
            out=id16[:], in_=id16[:], compare_op=OP.not_equal,
            fill=1.0, base=0, pattern=[[-1, P]], channel_multiplier=1),
            reads=[id16], writes=[id16])
        id16_tok = sch.last_w[Sched._key(id16)]

        def dma_w(dst, src_ap):
            return lambda eng: eng.dma_start(out=dst, in_=src_ap)
        tok_whg = sch.add("gpsimd", dma_w(r3(w_hg[:], "p (dc e) -> p dc e", dc=DC),
                                          r3(whg_h, "(dc p) e -> p dc e", p=P)),
                          writes=[w_hg], sem="w_s", inc=16)
        tok_wr = sch.add("gpsimd", dma_w(r3(w_r[:], "p (dc o) -> p dc o", dc=DC),
                                         r3(wr_h, "(dc p) o -> p dc o", p=P)),
                         writes=[w_r], sem="w_s", inc=16, hard=[tok_whg])
        tok_wk = sch.add("gpsimd", dma_w(r3(w_k[:], "p (dc o) -> p dc o", dc=DC),
                                         r3(wk_h, "(dc p) o -> p dc o", p=P)),
                         writes=[w_k], sem="w_s", inc=16, hard=[tok_wr])
        tok_wv = sch.add("gpsimd", dma_w(r3(w_v[:], "p (dc o) -> p dc o", dc=DC),
                                         r3(wv_h, "(dc p) o -> p dc o", p=P)),
                         writes=[w_v], sem="w_s", inc=16, hard=[tok_wk])

        def dma_mu(dst, src_ap):
            return lambda eng: eng.dma_start(out=dst, in_=src_ap,
                                             allow_slow_non_contiguous=True)
        tok_mur = sch.add("gpsimd", dma_mu(mur_sb[:], r3(mur_h, "(c p) -> p c", p=P)),
                          writes=[mur_sb], sem="w_s", inc=16, hard=[tok_wv])
        sch.add("gpsimd", dma_mu(muk_sb[:], r3(muk_h, "(c p) -> p c", p=P)),
                writes=[muk_sb], sem="w_s", inc=16, hard=[tok_mur])

        ps_free = [None, None, None, None]
        pt_free = [None, None]
        pb_free = [None, None]
        t_kind = [None, None]

        def t_switch(kind):
            prev_kind, prev_tok = t_kind
            t_kind[0] = kind
            if prev_kind is not None and prev_kind != kind:
                return [prev_tok]
            return []

        def t_done(tok):
            t_kind[1] = tok

        def x_load(m, stg):
            b, q = m // 4, m % 4
            src = r3(x_h[b, q * STK:(q + 1) * STK, :], "(st p) d -> p st d", p=P)
            dst = r3(x_tms[stg % 2][:], "p (st d) -> p st d", d=D)
            sch.add("sync", lambda eng, s=src, d=dst: eng.dma_start(out=d, in_=s),
                    writes=[x_tms[stg % 2]], sem="in_s", inc=16,
                    hard=[("in_s", 16 * stg)] if stg >= 1 else [])

        def ln_stats(src_t):
            src = src_t[:]
            def bn(eng):
                last = None
                for st in range(NST):
                    for hlf in range(2):
                        last = eng.bn_stats(
                            stats[:, st * 12 + hlf * 6: st * 12 + hlf * 6 + 6],
                            src[:, st * D + hlf * 512: st * D + (hlf + 1) * 512])
                return last
            t_bn0 = sch.add("vector", bn, reads=[src_t], writes=[stats])
            pe_warm([t_bn0])

            sv = r3(stats[:], "p (st g t) -> p st t g", st=4, g=4)
            means16 = r3(stats[:], "p (sg t) -> p sg t", t=3)[:, :, 1:2]
            rmean = r3(red1[:], "p (st t) -> p t st", t=3)
            sch.add("vector", lambda eng: eng.tensor_reduce(
                r3(red1[:], "p (st t o) -> p st t o", st=4, o=1), sv,
                axis=AX.X, op=OP.add), reads=[stats], writes=[red1])
            sch.add("vector", lambda eng: eng.tensor_tensor(
                out=r3(sqscr[:], "p (a o) -> p a o", o=1),
                in0=means16, in1=means16, op=OP.mult),
                reads=[stats], writes=[sqscr])
            sch.add("vector", lambda eng: eng.tensor_reduce(
                r3(t1v[:], "p (a o) -> p a o", o=1),
                r3(sqscr[:], "p (st g) -> p st g", g=4),
                axis=AX.X, op=OP.add), reads=[sqscr], writes=[t1v])
            sch.add("vector", lambda eng: eng.tensor_scalar(
                out=r3(negm[:], "p (o st) -> p o st", o=1),
                in0=rmean[:, 1:2, :], scalar1=-0.25, scalar2=None,
                op0=OP.mult), reads=[red1], writes=[negm])
            sch.add("vector", lambda eng: eng.tensor_tensor(
                out=msq[:], in0=negm[:], in1=negm[:], op=OP.mult),
                reads=[negm], writes=[msq])
            sch.add("vector", lambda eng: eng.tensor_scalar(
                out=sqscr[:, 0:4], in0=msq[:], scalar1=-EPS,
                scalar2=None, op0=OP.add), reads=[msq], writes=[sqscr])
            sch.add("vector", lambda eng: eng.scalar_tensor_tensor(
                out=t1v[:], in0=t1v[:], scalar=0.25, in1=sqscr[:, 0:4],
                op0=OP.mult, op1=OP.subtract),
                reads=[t1v, sqscr], writes=[t1v])
            sch.add("vector", lambda eng: eng.scalar_tensor_tensor(
                out=r3(varv[:], "p (o st) -> p o st", o=1),
                in0=rmean[:, 2:3, :], scalar=1.0 / 1024.0,
                in1=r3(t1v[:], "p (o st) -> p o st", o=1),
                op0=OP.mult, op1=OP.add),
                reads=[red1, t1v], writes=[varv])
            sch.add("scalar", lambda eng: eng.activation(
                out=t1v[:], in_=varv[:], func=AF.Ln),
                reads=[varv], writes=[t1v])
            t_rs = sch.add("scalar", lambda eng: eng.activation(
                out=rstd[:], in_=t1v[:], func=AF.Exp, scale=-0.5),
                reads=[t1v], writes=[rstd])
            pe_warm([t_rs])


        def ln_apply(dst_t, src_t, engine="vector"):
            dst, src = dst_t[:], src_t[:]
            dkeys = b2f if dst_t is B2 else [dst_t]
            skeys = b2f if src_t is B2 else [src_t]
            toks = []
            for st in range(NST):
                if engine == "scalar":
                    def apl(eng, st=st):
                        return eng.activation(
                            out=dst[:, st * D:(st + 1) * D],
                            in_=src[:, st * D:(st + 1) * D],
                            func=AF.Identity,
                            scale=rstd[:, st:st + 1], bias=nmr[:, st:st + 1])
                    toks.append(sch.add("scalar", apl,
                                        reads=skeys + [rstd, nmr], writes=dkeys))
                else:
                    def apl(eng, st=st):
                        return eng.tensor_scalar(
                            out=dst[:, st * D:(st + 1) * D],
                            in0=src[:, st * D:(st + 1) * D],
                            scalar1=negm[:, st:st + 1], scalar2=rstd[:, st:st + 1],
                            op0=OP.add, op1=OP.mult)
                    toks.append(sch.add("vector", apl,
                                        reads=skeys + [negm, rstd], writes=dkeys))
            return toks

        def transpose_tm_to_fm(src_t, dst_t, st_deps=None):
            nonlocal pb_free
            src, dst = src_t[:], dst_t[:]
            for st in range(NST):
                for half in range(2):
                    bank = half
                    def tr(eng, st=st, half=half, bank=bank, src=src):
                        last = None
                        for ci in range(4):
                            c = half * 4 + ci
                            last = eng.transpose(
                                pb[bank][:, ci * P:(ci + 1) * P],
                                src[:, st * D + c * P: st * D + (c + 1) * P],
                                id16[:])
                        return last
                    dep = [st_deps[st]] if st_deps else []
                    rd = [id16] if st_deps else [src_t, id16]
                    t_tr = sch.add("tensor", tr, reads=rd,
                                   war_reads=[src_t] if st_deps else [],
                                   deps=dep + [pb_free[bank]],
                                   hard=t_switch("tr"))
                    t_done(t_tr)
                    def ev(eng, st=st, half=half, bank=bank, dst=dst):
                        dv = r3(dst, "p (c s) -> p c s", c=DC)
                        return eng.copy(
                            out=dv[:, half * 4:(half + 1) * 4,
                                   st * P:(st + 1) * P],
                            in_=r3(pb[bank][:], "p (ci s) -> p ci s", ci=4))
                    t_ev = sch.add("scalar", ev,
                                   writes=[(dst_t, c)
                                           for c in range(half * 4, half * 4 + 4)],
                                   deps=[t_tr])
                    pb_free[bank] = t_ev

        def transpose_fm_to_tm(src_t, dst_t, fine=False):
            nonlocal pt_free
            src, dst = src_t[:], dst_t[:]
            for st in range(NST):
                for half in range(2):
                    bank = half
                    def tr(eng, st=st, half=half, bank=bank, src=src):
                        last = None
                        for ci in range(4):
                            c = half * 4 + ci
                            last = eng.transpose(
                                pt[bank][:, ci * P:(ci + 1) * P],
                                src[:, c * STK + st * P: c * STK + (st + 1) * P],
                                id32[:])
                        return last
                    rd = ([(src_t, c) for c in range(half * 4, half * 4 + 4)]
                          if fine else [src_t])
                    t_tr = sch.add("tensor", tr, reads=rd + [id32],
                                   deps=[pt_free[bank]], hard=t_switch("tr"))
                    t_done(t_tr)
                    def ev(eng, st=st, half=half, bank=bank, dst=dst):
                        return eng.copy(
                            out=dst[:, st * D + half * 512: st * D + (half + 1) * 512],
                            in_=pt[bank][:])
                    t_ev = sch.add("scalar", ev, writes=[dst_t], deps=[t_tr])
                    pt_free[bank] = t_ev

        def mm_block(w_t, w_stride, n_out, rhs_t, w_tok, evac_fns):
            nonlocal ps_free
            w_sb, rhs = w_t[:], rhs_t[:]
            for oc in range(n_out):
                bank = oc % 4
                def mm(eng, oc=oc, w_sb=w_sb, w_stride=w_stride, rhs=rhs, bank=bank):
                    last = None
                    for dc in range(DC):
                        last = eng.matmul(
                            ps[bank][:],
                            w_sb[:, dc * w_stride + oc * P: dc * w_stride + (oc + 1) * P],
                            rhs[:, dc * STK:(dc + 1) * STK],
                            start=(dc == 0), stop=(dc == DC - 1))
                    return last
                bank_deps = ps_free[bank] or []
                t_mm = sch.add("tensor", mm, reads=[w_t, rhs_t],
                               deps=[w_tok] + list(bank_deps),
                               hard=t_switch("mm"))
                t_done(t_mm)
                prev_ev = None
                for (engine, fn, rd, wr) in evac_fns(oc):
                    prev_ev = sch.add(
                        engine, (lambda eng, fn=fn, bank=bank: fn(eng, ps[bank][:])),
                        reads=list(rd), writes=list(wr),
                        deps=[t_mm] + ([prev_ev] if prev_ev else []))
                ps_free[bank] = [prev_ev]

        dbg_toks = {}
        def dbg_dump(name, buf, dt=F32):
            if not DBG:
                return
            t = nc.dram_tensor(name, [P, 4096], dt, kind="ExternalOutput").ap()
            sch.add("sync", lambda eng, t=t, buf=buf: eng.dma_start(out=t, in_=buf[:]),
                    reads=[buf], sem="out_s", inc=16)
            dbg_toks[name] = True


        ln1_toks = [None]
        ln3_toks = [None]

        def front_LN1(f, stg):
            ln_stats(x_tms[stg % 2])
            ln1_toks[0] = ln_apply(bfA, x_tms[stg % 2])

        def back_hT(m):
            transpose_fm_to_tm(zf, B1, fine=True)

        def front_xsT(f):
            transpose_tm_to_fm(bfA, xsf, st_deps=ln1_toks[0])

        def back_LN2(m, stg):
            x_tm = x_tms[stg % 2]
            ln_stats(B1)
            ln_apply(B2, B1, engine="vector")
            sch.add("vector", lambda eng, x_tm=x_tm: eng.tensor_tensor(
                out=x_tm[:], in0=x_tm[:], in1=B2[:], op=OP.add),
                reads=[x_tm] + b2f, writes=[x_tm])

        def front_attn_scan(f):
            nonlocal ps_free
            first = (f % 4 == 0)
            cb = (f // 4) * DC
            w_sb, rhs = w_hg[:], xsf[:]

            def mm_group(ec, bank):
                def mm(eng, ec=ec, bank=bank):
                    last = None
                    for dc in range(DC):
                        last = eng.matmul(
                            ps[bank][:],
                            w_sb[:, dc * E + ec * P: dc * E + (ec + 1) * P],
                            rhs[:, dc * STK:(dc + 1) * STK],
                            start=(dc == 0), stop=(dc == DC - 1))
                    return last
                bank_deps = ps_free[bank] or []
                return sch.add("tensor", mm,
                               reads=[w_hg] + [(xsf, cc) for cc in range(DC)],
                               deps=[tok_whg] + list(bank_deps),
                               hard=t_switch("mm"))

            for c in range(DC):
                bank_h = (2 * c) % 4
                bank_g = (2 * c + 1) % 4
                t_mm_h = mm_group(c, bank_h)
                t_done(t_mm_h)
                t_mm_g = mm_group(DC + c, bank_g)
                t_done(t_mm_g)
                sch.add("scalar", (lambda eng, c=c, bank=bank_h:
                    eng.activation(out=rf[:, c * STK:(c + 1) * STK],
                                   in_=ps[bank][:], func=AF.Relu)),
                    writes=[(rf, c)], deps=[t_mm_h])
                t_mn = sch.add("scalar", (lambda eng, c=c, bank=bank_h:
                    eng.activation(out=mf[:, c * STK:(c + 1) * STK],
                                   in_=ps[bank][:], func=AF.Relu, scale=-1.0)),
                    writes=[(mf, c)], deps=[t_mm_h])
                ps_free[bank_h] = [t_mn]
                t_z = sch.add("scalar", (lambda eng, c=c, bank=bank_g:
                    eng.activation(out=zf[:, c * STK:(c + 1) * STK],
                                   in_=ps[bank][:], func=AF.Sigmoid)),
                    writes=[(zf, c)], deps=[t_mm_g])
                ps_free[bank_g] = [t_z]
                sch.add("scalar", (lambda eng, c=c: eng.activation(
                    out=mf[:, c * STK:(c + 1) * STK],
                    in_=mf[:, c * STK:(c + 1) * STK], func=AF.Sigmoid,
                    scale=-1.0)),
                    reads=[(mf, c)], writes=[(mf, c)])
                sch.add("vector", (lambda eng, c=c: eng.tensor_tensor(
                    out=rf[:, c * STK:(c + 1) * STK],
                    in0=rf[:, c * STK:(c + 1) * STK],
                    in1=mf[:, c * STK:(c + 1) * STK], op=OP.add)),
                    reads=[(rf, c), (mf, c)], writes=[(rf, c)])
                sch.add("vector", (lambda eng, c=c: eng.tensor_tensor(
                    out=rf[:, c * STK:(c + 1) * STK],
                    in0=zf[:, c * STK:(c + 1) * STK],
                    in1=rf[:, c * STK:(c + 1) * STK], op=OP.mult)),
                    reads=[(zf, c), (rf, c)], writes=[(rf, c)])
                sch.add("vector", (lambda eng, c=c: eng.tensor_scalar(
                    out=mf[:, c * STK:(c + 1) * STK],
                    in0=zf[:, c * STK:(c + 1) * STK],
                    scalar1=-1.0, scalar2=1.0, op0=OP.mult, op1=OP.add)),
                    reads=[(zf, c), (mf, c)], writes=[(mf, c)])
                init = 0.0 if first else carry[:, cb + c:cb + c + 1]
                sch.add("vector", (lambda eng, c=c, init=init:
                    eng.tensor_tensor_scan(
                        out=zf[:, c * STK:(c + 1) * STK],
                        data0=mf[:, c * STK:(c + 1) * STK],
                        data1=rf[:, c * STK:(c + 1) * STK],
                        initial=init, op0=OP.mult, op1=OP.add)),
                    reads=[(mf, c), (rf, c)] + ([] if first else [carry]),
                    writes=[(zf, c)])
            sch.add("vector", lambda eng, cb=cb: eng.tensor_copy(
                r3(carry[:, cb:cb + DC], "p (c o) -> p c o", o=1),
                r3(zf[:], "p (c s) -> p c s", c=DC)[:, :, STK - 1:STK]),
                reads=[(zf, c) for c in range(DC)], writes=[carry])

        def back_LN3(m, stg):
            ln_stats(x_tms[stg % 2])
            ln3_toks[0] = ln_apply(bfA, x_tms[stg % 2])

        def back_yT(m):
            transpose_tm_to_fm(bfA, xsf, st_deps=ln3_toks[0])

        def mk_mix(m, mu_t, dst):
            yb = (m // 4) * DC
            mu_sb = mu_t[:]
            for c in range(DC):
                def mx(eng, c=c, mu_sb=mu_sb, yb=yb, dst=dst[:]):
                    eng.scalar_tensor_tensor(
                        out=dst[:, c * STK + 1:(c + 1) * STK],
                        in0=B2[:, c * STK + 1:(c + 1) * STK],
                        scalar=mu_sb[:, c:c + 1],
                        in1=xsf[:, c * STK:(c + 1) * STK - 1],
                        op0=OP.mult, op1=OP.add)
                    return eng.scalar_tensor_tensor(
                        out=dst[:, c * STK:c * STK + 1],
                        in0=B2[:, c * STK:c * STK + 1],
                        scalar=mu_sb[:, c:c + 1],
                        in1=ylast[:, yb + c:yb + c + 1],
                        op0=OP.mult, op1=OP.add)
                sch.add("vector", mx, reads=[(B2, c), (xsf, c), ylast, mu_t],
                        writes=[dst])

        def back_mix_k(m):
            first = (m % 4 == 0)
            yb = (m // 4) * DC
            if first:
                sch.add("vector", lambda eng, yb=yb: eng.memset(
                    ylast[:, yb:yb + DC], 0.0), writes=[ylast])
            xs3 = r3(xsf[:], "p (c s) -> p c s", c=DC)
            d3 = r3(B2[:], "p (c s) -> p c s", c=DC)
            sch.add("vector", lambda eng: eng.tensor_tensor(
                out=d3[:, :, 1:STK], in0=xs3[:, :, 1:STK],
                in1=xs3[:, :, 0:STK - 1], op=OP.subtract),
                reads=[(xsf, c) for c in range(DC)], writes=b2f)
            sch.add("vector", lambda eng, yb=yb: eng.tensor_tensor(
                out=d3[:, :, 0:1], in0=xs3[:, :, 0:1],
                in1=r3(ylast[:, yb:yb + DC], "p (c o) -> p c o", o=1),
                op=OP.subtract),
                reads=[(xsf, c) for c in range(DC)] + [ylast], writes=b2f)
            mk_mix(m, muk_sb, xkb)

        def back_mix_r(m):
            yb = (m // 4) * DC
            xs3 = r3(xsf[:], "p (c s) -> p c s", c=DC)
            mk_mix(m, mur_sb, bfA)
            sch.add("vector", lambda eng, yb=yb: eng.tensor_copy(
                r3(ylast[:, yb:yb + DC], "p (c o) -> p c o", o=1),
                xs3[:, :, STK - 1:STK]),
                reads=[(xsf, c) for c in range(DC)], writes=[ylast])
            def k_evacs(oc):
                return [("scalar", (lambda eng, bank_ap, oc=oc: eng.activation(
                    out=B2[:, oc * STK:(oc + 1) * STK], in_=bank_ap,
                    func=AF.Relu)), [], [(B2, oc)])]
            mm_block(w_k, D, DC, xkb, tok_wk, k_evacs)
            sch.add("scalar", lambda eng: eng.activation(
                out=xkb[:], in_=B2[:], func=AF.Square), reads=b2f, writes=[xkb])
            def r_evacs(oc):
                return [("scalar", (lambda eng, bank_ap, oc=oc: eng.activation(
                    out=B1[:, oc * STK:(oc + 1) * STK], in_=bank_ap,
                    func=AF.Sigmoid)), [], [B1])]
            mm_block(w_r, D, DC, bfA, tok_wr, r_evacs)

        def back_ka_kv(m):
            def kv_evacs(oc):
                return [("vector", (lambda eng, bank_ap, oc=oc: eng.tensor_tensor(
                    out=B2[:, oc * STK:(oc + 1) * STK],
                    in0=B1[:, oc * STK:(oc + 1) * STK], in1=bank_ap,
                    op=OP.mult)), [B1], [(B2, oc)])]
            mm_block(w_v, D, DC, xkb, tok_wv, kv_evacs)

        def back_oT(m):
            transpose_fm_to_tm(B2, B1, fine=True)

        def back_LN4_out(m, stg):
            x_tm = x_tms[stg % 2]
            b, q = m // 4, m % 4
            ln_stats(B1)
            ln_apply(B2, B1, engine="vector")
            sch.add("vector", lambda eng, x_tm=x_tm: eng.tensor_tensor(
                out=B2[:], in0=x_tm[:], in1=B2[:], op=OP.add),
                reads=[x_tm] + b2f, writes=b2f)
            dst = r3(y_h[b, q * STK:(q + 1) * STK, :], "(st p) d -> p st d", p=P)
            sch.add("sync", lambda eng, dst=dst: eng.dma_start(
                out=dst, in_=r3(B2[:], "p (st d) -> p st d", d=D)),
                reads=b2f, sem="out_s", inc=16,
                hard=[("out_s", 16 * stg)] if stg >= 1 else [])

        morder = [0, 4, 1, 5, 2, 6, 3, 7][:NM] if NM == 8 else list(range(NM))
        x_load(morder[0], 0)
        if NM > 1:
            x_load(morder[1], 1)
        for stg in range(NM + 1):
            f = morder[stg] if stg < NM else None
            m = morder[stg - 1] if stg >= 1 else None
            if m is not None:
                back_LN2(m, stg - 1)
            if f is not None:
                front_LN1(f, stg)
                front_xsT(f)
            if m is not None:
                back_LN3(m, stg - 1)
            if f is not None:
                front_attn_scan(f)
            if m is not None:
                back_yT(m)
                back_mix_k(m)
                back_mix_r(m)
                back_ka_kv(m)
                back_oT(m)
                back_LN4_out(m, stg - 1)
            if f is not None:
                back_hT(f)
            if m is not None and stg + 1 < NM:
                x_load(morder[stg + 1], stg + 1)

        sch.add("sync", lambda eng: None, deps=[("out_s", sch.count["out_s"])],
                sem="in_s", inc=0)

        with nc.Block() as block:
            @block.sync
            def _(eng):
                sch.emit("sync", eng, sems)

            @block.tensor
            def _(eng):
                sch.emit("tensor", eng, sems, warm=(id16[:], id16_tok))

            @block.vector
            def _(eng):
                sch.emit("vector", eng, sems)

            @block.scalar
            def _(eng):
                sch.emit("scalar", eng, sems)

            @block.gpsimd
            def _(eng):
                sch.emit("gpsimd", eng, sems)

    return nc


_NC_CACHE = None


def kernel(L0_attn_x, L0_attn_W_hg, L0_ffn_W_r, L0_ffn_W_k, L0_ffn_W_v,
           L0_ffn_mu_r, L0_ffn_mu_k):
    global _NC_CACHE
    x = np.ascontiguousarray(np.asarray(L0_attn_x, dtype=np.float32))
    whgT = np.ascontiguousarray(np.asarray(L0_attn_W_hg, dtype=np.float32).T)
    wrT = np.ascontiguousarray(np.asarray(L0_ffn_W_r, dtype=np.float32).T)
    wkT = np.ascontiguousarray(np.asarray(L0_ffn_W_k, dtype=np.float32).T)
    wvT = np.ascontiguousarray(np.asarray(L0_ffn_W_v, dtype=np.float32).T)
    mur = np.ascontiguousarray(np.asarray(L0_ffn_mu_r, dtype=np.float32))
    muk = np.ascontiguousarray(np.asarray(L0_ffn_mu_k, dtype=np.float32))

    if _NC_CACHE is None:
        _NC_CACHE = build_nc()
    nc = _NC_CACHE

    in_maps = []
    for core in range(8):
        in_maps.append({
            "x": x[core * BC:(core + 1) * BC],
            "whgT": whgT, "wrT": wrT, "wkT": wkT, "wvT": wvT,
            "mur": mur, "muk": muk,
        })
    res = run_bass_kernel_spmd(nc, in_maps, core_ids=list(range(8)))
    out = np.concatenate([r["y"] for r in res.results], axis=0)
    return out


# version 30 (final)
# speedup vs baseline: 1.5304x; 1.0337x over previous
import sys
sys.path.insert(0, "/opt/trn_rl_repo")
import contextlib
import numpy as np
import concourse.bass as bass
import concourse.mybir as mybir
from concourse.bass_utils import run_bass_kernel_spmd

F32 = mybir.dt.float32
BF16 = mybir.dt.bfloat16
AF = mybir.ActivationFunctionType
OP = mybir.AluOpType
AX = mybir.AxisListType

P = 128
D = 1024
E = 2048
S = 2048
BC = 2
NM = 8
STK = 512
NST = 4
DC = D // P
ECN = E // P
EPS = 1e-5
DBG = False
DBG_F = -1
SERIAL = False

ENGINES = ("sync", "tensor", "vector", "scalar", "gpsimd")
OWN_SEM = {"sync": "in_s", "tensor": "t_s", "vector": "v_s",
           "scalar": "s_s", "gpsimd": "g_s"}
ALL_SEMS = ("in_s", "out_s", "w_s", "g_s", "t_s", "v_s", "s_s")


class Sched:

    def __init__(self):
        self.ops = {e: [] for e in ENGINES}
        self.count = {s: 0 for s in ALL_SEMS}
        self.last_w = {}
        self.readers = {}

    @staticmethod
    def _key(buf):
        if isinstance(buf, tuple):
            return (id(buf[0]), buf[1])
        return id(buf)

    def add(self, engine, fn, reads=(), writes=(), deps=(), sem=None, inc=None,
            hard=(), war_reads=()):
        if sem is None:
            sem = OWN_SEM[engine]
        if inc is None:
            inc = 16 if engine == "sync" else 1
        deps = [d for d in deps if d is not None]
        for b in reads:
            t = self.last_w.get(self._key(b))
            if t is not None:
                deps.append(t)
        for b in writes:
            k = self._key(b)
            t = self.last_w.get(k)
            if t is not None:
                deps.append(t)
            for s, v in self.readers.get(k, {}).items():
                deps.append((s, v))
        hard = [d for d in hard if d is not None]
        self.count[sem] += inc
        tok = (sem, self.count[sem])
        self.ops[engine].append((tuple(deps), tuple(hard), fn, sem, inc))
        for b in list(reads) + list(war_reads):
            k = self._key(b)
            rd = self.readers.setdefault(k, {})
            rd[sem] = self.count[sem]
        for b in writes:
            k = self._key(b)
            if inc:
                self.last_w[k] = tok
            self.readers[k] = {}
        return tok

    def emit(self, engine, eng, sems, warm=None):
        waited = {}
        warm_ready = False
        for deps, hard, fn, sem, inc in self.ops[engine]:
            todo = {}
            for (s, v) in list(deps) + list(hard):
                if v > todo.get(s, 0):
                    todo[s] = v
            if warm is not None and todo:
                w_ap, w_tok = warm
                s_b, v_b, d_b = None, 0, 0
                for s, v in todo.items():
                    dd = v - waited.get(s, 0)
                    if dd > d_b:
                        s_b, v_b, d_b = s, v, dd
                step = 16 if s_b in ("in_s", "out_s", "w_s") else 2
                if d_b >= 3 * step:
                    if not warm_ready:
                        wi_s, wi_v = w_tok
                        if wi_v > waited.get(wi_s, 0):
                            eng.wait_ge(sems[wi_s], wi_v)
                            waited[wi_s] = wi_v
                        warm_ready = True
                    nb = min(8, d_b // step - 1)
                    for j in range(nb, 0, -1):
                        wv = v_b - j * step
                        if wv <= waited.get(s_b, 0):
                            continue
                        eng.wait_ge(sems[s_b], wv)
                        waited[s_b] = wv
                        eng.ldweights(w_ap)
            for s, v in sorted(todo.items()):
                if v <= waited.get(s, 0):
                    continue
                eng.wait_ge(sems[s], v)
                waited[s] = v
            inst = fn(eng)
            if inc and inst is not None:
                inst.then_inc(sems[sem], inc)


def r3(ap, pattern, **kw):
    return ap.rearrange(pattern, **kw)


def build_nc():
    nc = bass.Bass("TRN2", target_bir_lowering=False,
                   dynamic_dma_scratch_size=6144)
    x_h = nc.dram_tensor("x", [BC, S, D], F32, kind="ExternalInput").ap()
    whg_h = nc.dram_tensor("whgT", [D, E], F32, kind="ExternalInput").ap()
    wr_h = nc.dram_tensor("wrT", [D, D], F32, kind="ExternalInput").ap()
    wk_h = nc.dram_tensor("wkT", [D, D], F32, kind="ExternalInput").ap()
    wv_h = nc.dram_tensor("wvT", [D, D], F32, kind="ExternalInput").ap()
    mur_h = nc.dram_tensor("mur", [D], F32, kind="ExternalInput").ap()
    muk_h = nc.dram_tensor("muk", [D], F32, kind="ExternalInput").ap()
    y_h = nc.dram_tensor("y", [BC, S, D], F32, kind="ExternalOutput").ap()

    with contextlib.ExitStack() as es:
        sb = lambda name, shape, dt: es.enter_context(nc.sbuf_tensor(name, shape, dt))

        w_hg = sb("w_hg", [P, DC * E], BF16)
        w_r = sb("w_r", [P, DC * D], BF16)
        w_k = sb("w_k", [P, DC * D], BF16)
        w_v = sb("w_v", [P, DC * D], BF16)
        mur_sb = sb("mur_sb", [P, DC], F32)
        muk_sb = sb("muk_sb", [P, DC], F32)
        id32 = sb("id32", [P, P], F32)
        id16 = sb("id16", [P, P], BF16)
        x_tms = [sb("x_tm0", [P, NST * D], F32), sb("x_tm1", [P, NST * D], F32)]
        bfA = sb("bfA", [P, 4096], BF16)
        xsf = sb("xsf", [P, 4096], BF16)
        zf = sb("zf", [P, 4096], F32)
        rf = sb("rf", [P, 4096], F32)
        mf = sb("mf", [P, 4096], F32)
        B1 = sb("B1", [P, 4096], F32)
        B2 = sb("B2", [P, 4096], F32)
        xkb = sb("xkb", [P, 4096], BF16)
        stats = sb("stats", [P, 48], F32)
        red1 = sb("red1", [P, 12], F32)
        sqscr = sb("sqscr", [P, 16], F32)
        negm = sb("negm", [P, 4], F32)
        rstd = sb("rstd", [P, 4], F32)
        msq = sb("msq", [P, 4], F32)
        t1v = sb("t1v", [P, 4], F32)
        varv = sb("varv", [P, 4], F32)
        nmr = sb("nmr", [P, 4], F32)
        carry = sb("carry", [P, 2 * DC], F32)
        ylast = sb("ylast", [P, 2 * DC], BF16)

        ps = [es.enter_context(nc.psum_tensor(f"ps{i}", [P, 512], F32))
              for i in range(4)]
        pt = [es.enter_context(nc.psum_tensor(f"pt{i}", [P, 512], F32))
              for i in range(2)]
        pb = [es.enter_context(nc.psum_tensor(f"pb{i}", [P, 512], BF16))
              for i in range(2)]

        sems = {n: es.enter_context(nc.semaphore(n)) for n in ALL_SEMS}

        sch = Sched()
        b2f = [(B2, c) for c in range(DC)]

        def pe_warm(extra_deps=()):
            def dmm(eng):
                return eng.matmul(pt[0][:, 0:64], id16[:], id16[:, 0:64],
                                  start=True, stop=True)
            sch.add("tensor", dmm, reads=[id16],
                    deps=list(extra_deps) + [pt_free[0]])

        sch.add("gpsimd", lambda eng: eng.memset(id32[:], 0.0), writes=[id32])
        sch.add("gpsimd", lambda eng: eng.affine_select(
            out=id32[:], in_=id32[:], compare_op=OP.not_equal,
            fill=1.0, base=0, pattern=[[-1, P]], channel_multiplier=1),
            reads=[id32], writes=[id32])
        sch.add("gpsimd", lambda eng: eng.memset(id16[:], 0.0), writes=[id16])
        sch.add("gpsimd", lambda eng: eng.affine_select(
            out=id16[:], in_=id16[:], compare_op=OP.not_equal,
            fill=1.0, base=0, pattern=[[-1, P]], channel_multiplier=1),
            reads=[id16], writes=[id16])
        id16_tok = sch.last_w[Sched._key(id16)]

        def dma_w(dst, src_ap):
            return lambda eng: eng.dma_start(out=dst, in_=src_ap)
        tok_whg = sch.add("gpsimd", dma_w(r3(w_hg[:], "p (dc e) -> p dc e", dc=DC),
                                          r3(whg_h, "(dc p) e -> p dc e", p=P)),
                          writes=[w_hg], sem="w_s", inc=16)
        tok_wr = sch.add("gpsimd", dma_w(r3(w_r[:], "p (dc o) -> p dc o", dc=DC),
                                         r3(wr_h, "(dc p) o -> p dc o", p=P)),
                         writes=[w_r], sem="w_s", inc=16, hard=[tok_whg])
        tok_wk = sch.add("gpsimd", dma_w(r3(w_k[:], "p (dc o) -> p dc o", dc=DC),
                                         r3(wk_h, "(dc p) o -> p dc o", p=P)),
                         writes=[w_k], sem="w_s", inc=16, hard=[tok_wr])
        tok_wv = sch.add("gpsimd", dma_w(r3(w_v[:], "p (dc o) -> p dc o", dc=DC),
                                         r3(wv_h, "(dc p) o -> p dc o", p=P)),
                         writes=[w_v], sem="w_s", inc=16, hard=[tok_wk])

        def dma_mu(dst, src_ap):
            return lambda eng: eng.dma_start(out=dst, in_=src_ap,
                                             allow_slow_non_contiguous=True)
        tok_mur = sch.add("gpsimd", dma_mu(mur_sb[:], r3(mur_h, "(c p) -> p c", p=P)),
                          writes=[mur_sb], sem="w_s", inc=16, hard=[tok_wv])
        sch.add("gpsimd", dma_mu(muk_sb[:], r3(muk_h, "(c p) -> p c", p=P)),
                writes=[muk_sb], sem="w_s", inc=16, hard=[tok_mur])

        ps_free = [None, None, None, None]
        pt_free = [None, None]
        pb_free = [None, None]
        t_kind = [None, None]

        def t_switch(kind):
            prev_kind, prev_tok = t_kind
            t_kind[0] = kind
            if prev_kind is not None and prev_kind != kind:
                return [prev_tok]
            return []

        def t_done(tok):
            t_kind[1] = tok

        def x_load(m, stg):
            b, q = m // 4, m % 4
            src = r3(x_h[b, q * STK:(q + 1) * STK, :], "(st p) d -> p st d", p=P)
            dst = r3(x_tms[stg % 2][:], "p (st d) -> p st d", d=D)
            sch.add("sync", lambda eng, s=src, d=dst: eng.dma_start(out=d, in_=s),
                    writes=[x_tms[stg % 2]], sem="in_s", inc=16,
                    hard=[("in_s", 16 * stg)] if stg >= 1 else [])

        def ln_stats(src_t, fine=False):
            src = src_t[:]
            if fine:
                t_bn0 = None
                for st in range(NST):
                    for hlf in range(2):
                        def bn1(eng, st=st, hlf=hlf):
                            return eng.bn_stats(
                                stats[:, st * 12 + hlf * 6: st * 12 + hlf * 6 + 6],
                                src[:, st * D + hlf * 512: st * D + (hlf + 1) * 512])
                        t_bn0 = sch.add("vector", bn1,
                                        reads=[(src_t, st * 2 + hlf)],
                                        writes=[stats])
            else:
                def bn(eng):
                    last = None
                    for st in range(NST):
                        for hlf in range(2):
                            last = eng.bn_stats(
                                stats[:, st * 12 + hlf * 6: st * 12 + hlf * 6 + 6],
                                src[:, st * D + hlf * 512: st * D + (hlf + 1) * 512])
                    return last
                t_bn0 = sch.add("vector", bn, reads=[src_t], writes=[stats])
            pe_warm([t_bn0])

            sv = r3(stats[:], "p (st g t) -> p st t g", st=4, g=4)
            means16 = r3(stats[:], "p (sg t) -> p sg t", t=3)[:, :, 1:2]
            rmean = r3(red1[:], "p (st t) -> p t st", t=3)
            sch.add("vector", lambda eng: eng.tensor_reduce(
                r3(red1[:], "p (st t o) -> p st t o", st=4, o=1), sv,
                axis=AX.X, op=OP.add), reads=[stats], writes=[red1])
            sch.add("vector", lambda eng: eng.tensor_tensor(
                out=r3(sqscr[:], "p (a o) -> p a o", o=1),
                in0=means16, in1=means16, op=OP.mult),
                reads=[stats], writes=[sqscr])
            sch.add("vector", lambda eng: eng.tensor_reduce(
                r3(t1v[:], "p (a o) -> p a o", o=1),
                r3(sqscr[:], "p (st g) -> p st g", g=4),
                axis=AX.X, op=OP.add), reads=[sqscr], writes=[t1v])
            sch.add("vector", lambda eng: eng.tensor_scalar(
                out=r3(negm[:], "p (o st) -> p o st", o=1),
                in0=rmean[:, 1:2, :], scalar1=-0.25, scalar2=None,
                op0=OP.mult), reads=[red1], writes=[negm])
            sch.add("vector", lambda eng: eng.tensor_tensor(
                out=msq[:], in0=negm[:], in1=negm[:], op=OP.mult),
                reads=[negm], writes=[msq])
            sch.add("vector", lambda eng: eng.tensor_scalar(
                out=sqscr[:, 0:4], in0=msq[:], scalar1=-EPS,
                scalar2=None, op0=OP.add), reads=[msq], writes=[sqscr])
            sch.add("vector", lambda eng: eng.scalar_tensor_tensor(
                out=t1v[:], in0=t1v[:], scalar=0.25, in1=sqscr[:, 0:4],
                op0=OP.mult, op1=OP.subtract),
                reads=[t1v, sqscr], writes=[t1v])
            sch.add("vector", lambda eng: eng.scalar_tensor_tensor(
                out=r3(varv[:], "p (o st) -> p o st", o=1),
                in0=rmean[:, 2:3, :], scalar=1.0 / 1024.0,
                in1=r3(t1v[:], "p (o st) -> p o st", o=1),
                op0=OP.mult, op1=OP.add),
                reads=[red1, t1v], writes=[varv])
            sch.add("scalar", lambda eng: eng.activation(
                out=t1v[:], in_=varv[:], func=AF.Ln),
                reads=[varv], writes=[t1v])
            t_rs = sch.add("scalar", lambda eng: eng.activation(
                out=rstd[:], in_=t1v[:], func=AF.Exp, scale=-0.5),
                reads=[t1v], writes=[rstd])
            pe_warm([t_rs])


        def ln_apply(dst_t, src_t, engine="vector"):
            dst, src = dst_t[:], src_t[:]
            dkeys = b2f if dst_t is B2 else [dst_t]
            if src_t is B2:
                sk = lambda st: b2f
            elif src_t is B1:
                sk = lambda st: [(B1, 2 * st), (B1, 2 * st + 1)]
            else:
                sk = lambda st: [src_t]
            toks = []
            for st in range(NST):
                if engine == "scalar":
                    def apl(eng, st=st):
                        return eng.activation(
                            out=dst[:, st * D:(st + 1) * D],
                            in_=src[:, st * D:(st + 1) * D],
                            func=AF.Identity,
                            scale=rstd[:, st:st + 1], bias=nmr[:, st:st + 1])
                    toks.append(sch.add("scalar", apl,
                                        reads=sk(st) + [rstd, nmr], writes=dkeys))
                else:
                    def apl(eng, st=st):
                        return eng.tensor_scalar(
                            out=dst[:, st * D:(st + 1) * D],
                            in0=src[:, st * D:(st + 1) * D],
                            scalar1=negm[:, st:st + 1], scalar2=rstd[:, st:st + 1],
                            op0=OP.add, op1=OP.mult)
                    toks.append(sch.add("vector", apl,
                                        reads=sk(st) + [negm, rstd], writes=dkeys))
            return toks

        def transpose_tm_to_fm(src_t, dst_t, st_deps=None):
            nonlocal pb_free
            src, dst = src_t[:], dst_t[:]
            for st in range(NST):
                for half in range(2):
                    bank = half
                    def tr(eng, st=st, half=half, bank=bank, src=src):
                        last = None
                        for ci in range(4):
                            c = half * 4 + ci
                            last = eng.transpose(
                                pb[bank][:, ci * P:(ci + 1) * P],
                                src[:, st * D + c * P: st * D + (c + 1) * P],
                                id16[:])
                        return last
                    dep = [st_deps[st]] if st_deps else []
                    rd = [id16] if st_deps else [src_t, id16]
                    t_tr = sch.add("tensor", tr, reads=rd,
                                   war_reads=[src_t] if st_deps else [],
                                   deps=dep + [pb_free[bank]],
                                   hard=t_switch("tr"))
                    t_done(t_tr)
                    def ev(eng, st=st, half=half, bank=bank, dst=dst):
                        dv = r3(dst, "p (c s) -> p c s", c=DC)
                        return eng.copy(
                            out=dv[:, half * 4:(half + 1) * 4,
                                   st * P:(st + 1) * P],
                            in_=r3(pb[bank][:], "p (ci s) -> p ci s", ci=4))
                    t_ev = sch.add("scalar", ev,
                                   writes=[(dst_t, c)
                                           for c in range(half * 4, half * 4 + 4)],
                                   deps=[t_tr])
                    pb_free[bank] = t_ev

        def transpose_fm_to_tm(src_t, dst_t, fine=False):
            nonlocal pt_free
            src, dst = src_t[:], dst_t[:]
            for st in range(NST):
                for half in range(2):
                    bank = half
                    def tr(eng, st=st, half=half, bank=bank, src=src):
                        last = None
                        for ci in range(4):
                            c = half * 4 + ci
                            last = eng.transpose(
                                pt[bank][:, ci * P:(ci + 1) * P],
                                src[:, c * STK + st * P: c * STK + (st + 1) * P],
                                id32[:])
                        return last
                    rd = ([(src_t, c) for c in range(half * 4, half * 4 + 4)]
                          if fine else [src_t])
                    t_tr = sch.add("tensor", tr, reads=rd + [id32],
                                   deps=[pt_free[bank]], hard=t_switch("tr"))
                    t_done(t_tr)
                    def ev(eng, st=st, half=half, bank=bank, dst=dst):
                        return eng.copy(
                            out=dst[:, st * D + half * 512: st * D + (half + 1) * 512],
                            in_=pt[bank][:])
                    t_ev = sch.add("scalar", ev, writes=[(dst_t, st * 2 + half)],
                                   deps=[t_tr])
                    pt_free[bank] = t_ev

        def mm_block(w_t, w_stride, n_out, rhs_t, w_tok, evac_fns):
            nonlocal ps_free
            w_sb, rhs = w_t[:], rhs_t[:]
            for oc in range(n_out):
                bank = oc % 4
                def mm(eng, oc=oc, w_sb=w_sb, w_stride=w_stride, rhs=rhs, bank=bank):
                    last = None
                    for dc in range(DC):
                        last = eng.matmul(
                            ps[bank][:],
                            w_sb[:, dc * w_stride + oc * P: dc * w_stride + (oc + 1) * P],
                            rhs[:, dc * STK:(dc + 1) * STK],
                            start=(dc == 0), stop=(dc == DC - 1))
                    return last
                bank_deps = ps_free[bank] or []
                t_mm = sch.add("tensor", mm, reads=[w_t, rhs_t],
                               deps=[w_tok] + list(bank_deps),
                               hard=t_switch("mm"))
                t_done(t_mm)
                prev_ev = None
                for (engine, fn, rd, wr) in evac_fns(oc):
                    prev_ev = sch.add(
                        engine, (lambda eng, fn=fn, bank=bank: fn(eng, ps[bank][:])),
                        reads=list(rd), writes=list(wr),
                        deps=[t_mm] + ([prev_ev] if prev_ev else []))
                ps_free[bank] = [prev_ev]

        dbg_toks = {}
        def dbg_dump(name, buf, dt=F32):
            if not DBG:
                return
            t = nc.dram_tensor(name, [P, 4096], dt, kind="ExternalOutput").ap()
            sch.add("sync", lambda eng, t=t, buf=buf: eng.dma_start(out=t, in_=buf[:]),
                    reads=[buf], sem="out_s", inc=16)
            dbg_toks[name] = True


        ln1_toks = [None]
        ln3_toks = [None]

        def front_LN1(f, stg):
            ln_stats(x_tms[stg % 2])
            ln1_toks[0] = ln_apply(bfA, x_tms[stg % 2])

        def back_hT(m):
            transpose_fm_to_tm(zf, B1, fine=True)

        def front_xsT(f):
            transpose_tm_to_fm(bfA, xsf, st_deps=ln1_toks[0])

        def back_LN2(m, stg):
            x_tm = x_tms[stg % 2]
            ln_stats(B1, fine=True)
            ln_apply(B2, B1, engine="vector")
            sch.add("vector", lambda eng, x_tm=x_tm: eng.tensor_tensor(
                out=x_tm[:], in0=x_tm[:], in1=B2[:], op=OP.add),
                reads=[x_tm] + b2f, writes=[x_tm])

        def front_attn_scan(f):
            nonlocal ps_free
            first = (f % 4 == 0)
            cb = (f // 4) * DC
            w_sb, rhs = w_hg[:], xsf[:]

            def mm_group(ec, bank):
                def mm(eng, ec=ec, bank=bank):
                    last = None
                    for dc in range(DC):
                        last = eng.matmul(
                            ps[bank][:],
                            w_sb[:, dc * E + ec * P: dc * E + (ec + 1) * P],
                            rhs[:, dc * STK:(dc + 1) * STK],
                            start=(dc == 0), stop=(dc == DC - 1))
                    return last
                bank_deps = ps_free[bank] or []
                return sch.add("tensor", mm,
                               reads=[w_hg] + [(xsf, cc) for cc in range(DC)],
                               deps=[tok_whg] + list(bank_deps),
                               hard=t_switch("mm"))

            for c in range(DC):
                bank_h = (2 * c) % 4
                bank_g = (2 * c + 1) % 4
                t_mm_h = mm_group(c, bank_h)
                t_done(t_mm_h)
                t_mm_g = mm_group(DC + c, bank_g)
                t_done(t_mm_g)
                sch.add("scalar", (lambda eng, c=c, bank=bank_h:
                    eng.activation(out=rf[:, c * STK:(c + 1) * STK],
                                   in_=ps[bank][:], func=AF.Relu)),
                    writes=[(rf, c)], deps=[t_mm_h])
                t_mn = sch.add("scalar", (lambda eng, c=c, bank=bank_h:
                    eng.activation(out=mf[:, c * STK:(c + 1) * STK],
                                   in_=ps[bank][:], func=AF.Relu, scale=-1.0)),
                    writes=[(mf, c)], deps=[t_mm_h])
                ps_free[bank_h] = [t_mn]
                t_z = sch.add("scalar", (lambda eng, c=c, bank=bank_g:
                    eng.activation(out=zf[:, c * STK:(c + 1) * STK],
                                   in_=ps[bank][:], func=AF.Sigmoid)),
                    writes=[(zf, c)], deps=[t_mm_g])
                ps_free[bank_g] = [t_z]
                sch.add("scalar", (lambda eng, c=c: eng.activation(
                    out=mf[:, c * STK:(c + 1) * STK],
                    in_=mf[:, c * STK:(c + 1) * STK], func=AF.Sigmoid,
                    scale=-1.0)),
                    reads=[(mf, c)], writes=[(mf, c)])
                sch.add("vector", (lambda eng, c=c: eng.tensor_tensor(
                    out=rf[:, c * STK:(c + 1) * STK],
                    in0=rf[:, c * STK:(c + 1) * STK],
                    in1=mf[:, c * STK:(c + 1) * STK], op=OP.add)),
                    reads=[(rf, c), (mf, c)], writes=[(rf, c)])
                sch.add("vector", (lambda eng, c=c: eng.tensor_tensor(
                    out=rf[:, c * STK:(c + 1) * STK],
                    in0=zf[:, c * STK:(c + 1) * STK],
                    in1=rf[:, c * STK:(c + 1) * STK], op=OP.mult)),
                    reads=[(zf, c), (rf, c)], writes=[(rf, c)])
                sch.add("vector", (lambda eng, c=c: eng.tensor_scalar(
                    out=mf[:, c * STK:(c + 1) * STK],
                    in0=zf[:, c * STK:(c + 1) * STK],
                    scalar1=-1.0, scalar2=1.0, op0=OP.mult, op1=OP.add)),
                    reads=[(zf, c), (mf, c)], writes=[(mf, c)])
                init = 0.0 if first else carry[:, cb + c:cb + c + 1]
                sch.add("vector", (lambda eng, c=c, init=init:
                    eng.tensor_tensor_scan(
                        out=zf[:, c * STK:(c + 1) * STK],
                        data0=mf[:, c * STK:(c + 1) * STK],
                        data1=rf[:, c * STK:(c + 1) * STK],
                        initial=init, op0=OP.mult, op1=OP.add)),
                    reads=[(mf, c), (rf, c)] + ([] if first else [carry]),
                    writes=[(zf, c)])
            sch.add("vector", lambda eng, cb=cb: eng.tensor_copy(
                r3(carry[:, cb:cb + DC], "p (c o) -> p c o", o=1),
                r3(zf[:], "p (c s) -> p c s", c=DC)[:, :, STK - 1:STK]),
                reads=[(zf, c) for c in range(DC)], writes=[carry])

        def back_LN3(m, stg):
            ln_stats(x_tms[stg % 2])
            ln3_toks[0] = ln_apply(bfA, x_tms[stg % 2])

        def back_yT(m):
            transpose_tm_to_fm(bfA, xsf, st_deps=ln3_toks[0])

        def mk_mix(m, mu_t, dst):
            yb = (m // 4) * DC
            mu_sb = mu_t[:]
            for c in range(DC):
                def mx(eng, c=c, mu_sb=mu_sb, yb=yb, dst=dst[:]):
                    eng.scalar_tensor_tensor(
                        out=dst[:, c * STK + 1:(c + 1) * STK],
                        in0=B2[:, c * STK + 1:(c + 1) * STK],
                        scalar=mu_sb[:, c:c + 1],
                        in1=xsf[:, c * STK:(c + 1) * STK - 1],
                        op0=OP.mult, op1=OP.add)
                    return eng.scalar_tensor_tensor(
                        out=dst[:, c * STK:c * STK + 1],
                        in0=B2[:, c * STK:c * STK + 1],
                        scalar=mu_sb[:, c:c + 1],
                        in1=ylast[:, yb + c:yb + c + 1],
                        op0=OP.mult, op1=OP.add)
                sch.add("vector", mx, reads=[(B2, c), (xsf, c), ylast, mu_t],
                        writes=[dst])

        def back_mix_k(m):
            first = (m % 4 == 0)
            yb = (m // 4) * DC
            if first:
                sch.add("vector", lambda eng, yb=yb: eng.memset(
                    ylast[:, yb:yb + DC], 0.0), writes=[ylast])
            xs3 = r3(xsf[:], "p (c s) -> p c s", c=DC)
            d3 = r3(B2[:], "p (c s) -> p c s", c=DC)
            sch.add("vector", lambda eng: eng.tensor_tensor(
                out=d3[:, :, 1:STK], in0=xs3[:, :, 1:STK],
                in1=xs3[:, :, 0:STK - 1], op=OP.subtract),
                reads=[(xsf, c) for c in range(DC)], writes=b2f)
            sch.add("vector", lambda eng, yb=yb: eng.tensor_tensor(
                out=d3[:, :, 0:1], in0=xs3[:, :, 0:1],
                in1=r3(ylast[:, yb:yb + DC], "p (c o) -> p c o", o=1),
                op=OP.subtract),
                reads=[(xsf, c) for c in range(DC)] + [ylast], writes=b2f)
            mk_mix(m, muk_sb, xkb)

        def back_mix_r(m):
            yb = (m // 4) * DC
            xs3 = r3(xsf[:], "p (c s) -> p c s", c=DC)
            mk_mix(m, mur_sb, bfA)
            sch.add("vector", lambda eng, yb=yb: eng.tensor_copy(
                r3(ylast[:, yb:yb + DC], "p (c o) -> p c o", o=1),
                xs3[:, :, STK - 1:STK]),
                reads=[(xsf, c) for c in range(DC)], writes=[ylast])
            def k_evacs(oc):
                return [("scalar", (lambda eng, bank_ap, oc=oc: eng.activation(
                    out=B2[:, oc * STK:(oc + 1) * STK], in_=bank_ap,
                    func=AF.Relu)), [], [(B2, oc)])]
            mm_block(w_k, D, DC, xkb, tok_wk, k_evacs)
            sch.add("scalar", lambda eng: eng.activation(
                out=xkb[:], in_=B2[:], func=AF.Square), reads=b2f, writes=[xkb])
            def r_evacs(oc):
                return [("scalar", (lambda eng, bank_ap, oc=oc: eng.activation(
                    out=B1[:, oc * STK:(oc + 1) * STK], in_=bank_ap,
                    func=AF.Sigmoid)), [], [(B1, oc)])]
            mm_block(w_r, D, DC, bfA, tok_wr, r_evacs)

        def back_ka_kv(m):
            def kv_evacs(oc):
                return [("vector", (lambda eng, bank_ap, oc=oc: eng.tensor_tensor(
                    out=B2[:, oc * STK:(oc + 1) * STK],
                    in0=B1[:, oc * STK:(oc + 1) * STK], in1=bank_ap,
                    op=OP.mult)), [(B1, oc)], [(B2, oc)])]
            mm_block(w_v, D, DC, xkb, tok_wv, kv_evacs)

        def back_oT(m):
            transpose_fm_to_tm(B2, B1, fine=True)

        def back_LN4_out(m, stg):
            x_tm = x_tms[stg % 2]
            b, q = m // 4, m % 4
            ln_stats(B1, fine=True)
            ln_apply(B2, B1, engine="vector")
            sch.add("vector", lambda eng, x_tm=x_tm: eng.tensor_tensor(
                out=B2[:], in0=x_tm[:], in1=B2[:], op=OP.add),
                reads=[x_tm] + b2f, writes=b2f)
            dst = r3(y_h[b, q * STK:(q + 1) * STK, :], "(st p) d -> p st d", p=P)
            sch.add("sync", lambda eng, dst=dst: eng.dma_start(
                out=dst, in_=r3(B2[:], "p (st d) -> p st d", d=D)),
                reads=b2f, sem="out_s", inc=16,
                hard=[("out_s", 16 * stg)] if stg >= 1 else [])

        morder = [0, 4, 1, 5, 2, 6, 3, 7][:NM] if NM == 8 else list(range(NM))
        x_load(morder[0], 0)
        if NM > 1:
            x_load(morder[1], 1)
        for stg in range(NM + 1):
            f = morder[stg] if stg < NM else None
            m = morder[stg - 1] if stg >= 1 else None
            if m is not None:
                back_LN2(m, stg - 1)
            if f is not None:
                front_LN1(f, stg)
                front_xsT(f)
            if m is not None:
                back_LN3(m, stg - 1)
            if f is not None:
                front_attn_scan(f)
            if m is not None:
                back_yT(m)
                back_mix_k(m)
                back_mix_r(m)
                back_ka_kv(m)
                back_oT(m)
                back_LN4_out(m, stg - 1)
            if f is not None:
                back_hT(f)
            if m is not None and stg + 1 < NM:
                x_load(morder[stg + 1], stg + 1)

        sch.add("sync", lambda eng: None, deps=[("out_s", sch.count["out_s"])],
                sem="in_s", inc=0)

        with nc.Block() as block:
            @block.sync
            def _(eng):
                sch.emit("sync", eng, sems)

            @block.tensor
            def _(eng):
                sch.emit("tensor", eng, sems, warm=(id16[:], id16_tok))

            @block.vector
            def _(eng):
                sch.emit("vector", eng, sems)

            @block.scalar
            def _(eng):
                sch.emit("scalar", eng, sems)

            @block.gpsimd
            def _(eng):
                sch.emit("gpsimd", eng, sems)

    return nc


_NC_CACHE = None


def kernel(L0_attn_x, L0_attn_W_hg, L0_ffn_W_r, L0_ffn_W_k, L0_ffn_W_v,
           L0_ffn_mu_r, L0_ffn_mu_k):
    global _NC_CACHE
    x = np.ascontiguousarray(np.asarray(L0_attn_x, dtype=np.float32))
    whgT = np.ascontiguousarray(np.asarray(L0_attn_W_hg, dtype=np.float32).T)
    wrT = np.ascontiguousarray(np.asarray(L0_ffn_W_r, dtype=np.float32).T)
    wkT = np.ascontiguousarray(np.asarray(L0_ffn_W_k, dtype=np.float32).T)
    wvT = np.ascontiguousarray(np.asarray(L0_ffn_W_v, dtype=np.float32).T)
    mur = np.ascontiguousarray(np.asarray(L0_ffn_mu_r, dtype=np.float32))
    muk = np.ascontiguousarray(np.asarray(L0_ffn_mu_k, dtype=np.float32))

    if _NC_CACHE is None:
        _NC_CACHE = build_nc()
    nc = _NC_CACHE

    in_maps = []
    for core in range(8):
        in_maps.append({
            "x": x[core * BC:(core + 1) * BC],
            "whgT": whgT, "wrT": wrT, "wkT": wkT, "wvT": wvT,
            "mur": mur, "muk": muk,
        })
    res = run_bass_kernel_spmd(nc, in_maps, core_ids=list(range(8)))
    out = np.concatenate([r["y"] for r in res.results], axis=0)
    return out
